# version 02
# baseline (speedup 1.0000x reference)
import numpy as np
import ml_dtypes
from contextlib import ExitStack

import concourse.bass as bass
import concourse.mybir as mybir
from concourse.bass_utils import run_bass_kernel_spmd

F32 = mybir.dt.float32
BF16 = mybir.dt.bfloat16
AF = mybir.ActivationFunctionType
ALU = mybir.AluOpType
AX = mybir.AxisListType
bfdt = ml_dtypes.bfloat16

B, L, H, KD, FF = 2, 4096, 2048, 128, 5632
T = 1024
PRE = 3072
PC = 512
NPC = PRE // PC
C = 64
NCH = T // C
HT = H // 128
TT = T // 128
FT = FF // 128
DT = H // 128
OS = 256
NOS = H // OS
SCALE = float(KD) ** -0.5
EPS = 1e-5
QCNT = float(FF * H)
GT = 1024

_DEBUG_TAPS = False


class Sem:
    def __init__(self, ctx, nc, name):
        self.h = ctx.enter_context(nc.semaphore(name))
        self.v = 0

    def inc(self, inst, n=1):
        inst.then_inc(self.h, n)
        self.v += n
        return self.v

    def dma(self, inst):
        return self.inc(inst, 16)


def _build():
    nc = bass.Bass("TRN2", target_bir_lowering=False)

    d_x = nc.dram_tensor("x_seg", [T, H], F32, kind="ExternalInput").ap()
    d_xT = nc.dram_tensor("x_segT", [H, T], F32, kind="ExternalInput").ap()
    d_xp = nc.dram_tensor("x_pre", [PRE, H], F32, kind="ExternalInput").ap()
    d_xpT = nc.dram_tensor("x_preT", [H, PRE], F32, kind="ExternalInput").ap()
    d_wq = nc.dram_tensor("wqT", [H, KD], F32, kind="ExternalInput").ap()
    d_wk = nc.dram_tensor("wkT", [H, KD], F32, kind="ExternalInput").ap()
    d_wog = nc.dram_tensor("wogT", [H, H], F32, kind="ExternalInput").ap()
    d_wig = nc.dram_tensor("wigT", [H, H], F32, kind="ExternalInput").ap()
    d_w1 = nc.dram_tensor("w1T", [H, FF], F32, kind="ExternalInput").ap()
    d_w2 = nc.dram_tensor("w2T", [H, FF], F32, kind="ExternalInput").ap()
    d_w3 = nc.dram_tensor("w3T", [FF, H], F32, kind="ExternalInput").ap()
    d_gw = nc.dram_tensor("g_w", [H], F32, kind="ExternalInput").ap()
    d_id = nc.dram_tensor("identb", [128, 128], BF16, kind="ExternalInput").ap()
    d_mask = nc.dram_tensor("masksb", [64, 64], F32, kind="ExternalInput").ap()
    d_y = nc.dram_tensor("y_out", [H, T], F32, kind="ExternalOutput").ap()

    dbg = {}
    if _DEBUG_TAPS:
        dbg["S"] = nc.dram_tensor("dbg_S", [KD, H], F32, kind="ExternalOutput").ap()
        dbg["o"] = nc.dram_tensor("dbg_o", [T, H], F32, kind="ExternalOutput").ap()
        dbg["y1"] = nc.dram_tensor("dbg_y1", [T, H], F32, kind="ExternalOutput").ap()
        dbg["th"] = nc.dram_tensor("dbg_th", [128, 8], F32, kind="ExternalOutput").ap()
        dbg["ksig"] = nc.dram_tensor("dbg_ksig", [KD, T], F32, kind="ExternalOutput").ap()
        dbg["siluq"] = nc.dram_tensor("dbg_siluq", [KD, T], F32, kind="ExternalOutput").ap()
        dbg["h0"] = nc.dram_tensor("dbg_h0", [128, T], F32, kind="ExternalOutput").ap()
        dbg["z2"] = nc.dram_tensor("dbg_z2", [128, T], F32, kind="ExternalOutput").ap()

    ctx = ExitStack()
    with ctx:
        def sbt(name, shape, dt):
            return ctx.enter_context(nc.sbuf_tensor(name, shape, dt))

        wq_sb = sbt("wq_sb", [128, HT, 128], BF16)
        wk_sb = sbt("wk_sb", [128, HT, 128], BF16)
        gw_b = sbt("gw_b", [128, H], BF16)
        id_sb = sbt("id_sb", [128, 128], BF16)
        mask_sb = sbt("mask_sb", [64, 64], F32)
        ones_cb = sbt("ones_cb", [128, 1], BF16)
        ones_cf = sbt("ones_cf", [128, 1], F32)
        ones_rf = sbt("ones_rf", [1, 128], F32)
        cst = sbt("cst", [128, 3], F32)
        nc.const_aps.register(F32, EPS, cst[:, 0:1])
        nc.const_aps.register(F32, 0.5e-8, cst[:, 1:2])
        nc.const_aps.register(F32, -0.5e-8, cst[:, 2:3])

        rbig = sbt("rbig", [128, 45056], BF16)
        xT = rbig[:, 0:16384].rearrange("p (a t) -> p a t", a=HT)
        xtok = rbig[:, 16384:32768].rearrange("p (a t) -> p a t", a=TT)
        oTv = rbig[:, 16384:32768].rearrange("p (a t) -> p a t", a=HT)
        qp = rbig[:, 32768:33792]
        kp = rbig[:, 33792:34816]
        kpp = rbig[:, 34816:35840]
        S_sb = rbig[:, 35840:37888]
        vbuf0 = rbig[:, 37888:39936]
        vbuf1 = rbig[:, 39936:41984]
        kppT_sb = rbig[:, 41984:42112]
        P_sb = rbig[:, 42112:42176]
        kppTp = rbig[:, 42176:42688].rearrange("p (a t) -> p a t", a=4)
        kppre = rbig[:, 42688:43200]
        wogB = rbig[:, 32768:36864].rearrange("p (a t) -> p a t", a=HT)
        wigB = rbig[:, 36864:40960].rearrange("p (a t) -> p a t", a=HT)
        h_sb = rbig[:, 0:45056].rearrange("p (a t) -> p a t", a=FT)

        rb2 = sbt("rb2", [128, 16384], BF16)
        o_tok = rb2[:, 0:16384].rearrange("p (a t) -> p a t", a=TT)
        vpre = rb2[:, 0:8192].rearrange("p (a t) -> p a t", a=4)
        xpreT = rb2[:, 8192:16384].rearrange("p (a t) -> p a t", a=HT)
        wq1_sb = rb2[:, 0:2048].rearrange("p (a t) -> p a t", a=HT)
        wq2_sb = rb2[:, 2048:4096].rearrange("p (a t) -> p a t", a=HT)
        pq8 = rb2[:, 4096:6144]
        w3qA = rb2[:, 4096:9728].rearrange("p (a t) -> p a t", a=FT)
        w3qB = rb2[:, 9728:15360].rearrange("p (a t) -> p a t", a=FT)
        pq9 = rb2[:, 0:2048]
        junk9 = rb2[:, 15360:16384]

        rh = sbt("rh", [128, 16384], BF16)
        y1T = rh[:, 0:16384].rearrange("p (a t) -> p a t", a=HT)
        z2 = y1T
        junk_b = rh[:, 0:2048]
        og6 = rh[:, 2048:2304]
        wogA = rh[:, 4096:8192].rearrange("p (a t) -> p a t", a=HT)
        wigA = rh[:, 8192:12288].rearrange("p (a t) -> p a t", a=HT)

        rf = sbt("rf", [128, 10752], F32)
        siluq = rf[:, 0:1024]
        ksig = rf[:, 1024:2048]
        sp = rf[:, 2048:3072]
        ccum = rf[:, 3072:4096]
        E = rf[:, 4096:5120]
        Einv = rf[:, 5120:6144]
        kspre = rf[:, 6144:6656]
        sppre = rf[:, 6656:7168]
        ccpre = rf[:, 7168:7680]
        kfac = rf[:, 7680:8192]
        gam0 = rf[:, 8192:9216]
        gam1 = rf[:, 9216:10240]
        stats = rf[:, 10240:10752]
        gpart = stats[:, 0:288]
        th_c = stats[:, 288:292]
        nth_c = stats[:, 292:296]
        oss = stats[:, 296:304]
        rstdo = stats[:, 304:312]
        lnt = stats[:, 312:320]
        sum_c = stats[:, 320:328]
        sq_c = stats[:, 328:336]
        mean_c = stats[:, 336:344]
        e2_c = stats[:, 344:352]
        m2_c = stats[:, 352:360]
        nm2_c = stats[:, 360:368]
        r_c = stats[:, 368:376]
        dcol = stats[:, 376:377]
        ncol = stats[:, 377:378]
        grow = stats[:, 384:388]
        throw = stats[:, 388:392]
        nthrow = stats[:, 392:396]
        xres0 = rf[:, 0:2048]
        xres1 = rf[:, 2048:4096]
        sg0 = rf[:, 4096:4352]
        sg1 = rf[:, 4352:4608]
        w1rA = rf[:, 0:2048].rearrange("p (a t) -> p a t", a=HT)
        w2rA = rf[:, 2048:4096].rearrange("p (a t) -> p a t", a=HT)
        w1rB = rf[:, 4096:6144].rearrange("p (a t) -> p a t", a=HT)
        w2rB = rf[:, 6144:8192].rearrange("p (a t) -> p a t", a=HT)
        sgf0 = rf[:, 8192:8704]
        sgf1 = rf[:, 8704:9216]
        w3r = rf[:, 0:5632].rearrange("p (a t) -> p a t", a=FT)
        mb_sb = rf[:, 5632:6656]
        rb_sb = rf[:, 6656:7680]
        y2t = rf[:, 7680:8704]
        sums_r = rf[:, 2048:3072]
        ssq_r = rf[:, 3072:4096]
        ln2a = rf[:, 0:1024]
        ln2b = rf[:, 1024:2048]

        ps = ctx.enter_context(nc.psum_tensor("ps", [128, 4096], F32))

        def bank(i):
            return ps[:, i * 512:(i + 1) * 512]

        psbf = ps[:, 3584:4096].bitcast(BF16)

        g_init = Sem(ctx, nc, "g_init")
        g_pre = Sem(ctx, nc, "g_pre")
        g_vb0 = Sem(ctx, nc, "g_vb0")
        g_vb1 = Sem(ctx, nc, "g_vb1")
        g_wogA = Sem(ctx, nc, "g_wogA")
        g_wogB = Sem(ctx, nc, "g_wogB")
        g_xr0 = Sem(ctx, nc, "g_xr0")
        g_xr1 = Sem(ctx, nc, "g_xr1")
        s_const = Sem(ctx, nc, "s_const")
        s_gam0 = Sem(ctx, nc, "s_gam0")
        s_gam1 = Sem(ctx, nc, "s_gam1")
        s_w12A = Sem(ctx, nc, "s_w12A")
        s_w12B = Sem(ctx, nc, "s_w12B")
        s_w3 = Sem(ctx, nc, "s_w3")
        o_dma = Sem(ctx, nc, "o_dma")
        d_dbg = Sem(ctx, nc, "d_dbg")
        pe = Sem(ctx, nc, "pe")
        ve = Sem(ctx, nc, "ve")
        sc = Sem(ctx, nc, "sc")
        gpc = Sem(ctx, nc, "gpc")
        ck = {}

        blk = ctx.enter_context(nc.Block())

        gjobs = []
        gcols = []
        for m, srcd in enumerate((d_w1, d_w2, d_w3)):
            r = srcd.rearrange("(a p) f -> p a f", p=128)
            fdim = r.shape[2]
            na = r.shape[1]
            c0 = len(gjobs)
            for a in range(na):
                off = 0
                while off < fdim:
                    w = min(GT, fdim - off)
                    gjobs.append((m, r[:, a, off:off + w], w))
                    off += w
            gcols.append((c0, len(gjobs)))
        NGALL = len(gjobs)
        assert NGALL <= 288
        gamma_n = [0]

        def emit_gamma(k):
            for _ in range(k):
                n = gamma_n[0]
                if n >= NGALL:
                    return
                m, gsrc, w = gjobs[n]
                buf = (gam0 if n % 2 == 0 else gam1)[:, 0:w]
                gsem = s_gam0 if n % 2 == 0 else s_gam1
                col = gpart[:, n:n + 1]

                def gdma(sy, n=n, gsrc=gsrc, buf=buf, gsem=gsem):
                    if n >= 2:
                        sy.wait_ge(sc.h, ck[f"g_red{n - 2}"])
                    gsem.dma(sy.dma_start(buf, gsrc))
                    ck[f"g_ld{n}"] = gsem.v

                blk.sync(gdma)

                def gred(s, n=n, buf=buf, col=col, w=w, gsem=gsem):
                    s.wait_ge(gsem.h, ck[f"g_ld{n}"])
                    s.wait_ge(sc.h, sc.v)
                    sc.inc(s.activation(junk_b[:, 0:w], buf, AF.Abs, accum_out=col))
                    ck[f"g_red{n}"] = sc.v

                blk.scalar(gred)
                gamma_n[0] += 1

        def ph0_gp(gp):
            g_init.dma(gp.dma_start(wq_sb[:], d_wq.rearrange("(a p) k -> p a k", p=128)))
            g_init.dma(gp.dma_start(wk_sb[:], d_wk.rearrange("(a p) k -> p a k", p=128)))
            g_init.dma(gp.dma_start(gw_b[:], bass.AP(d_gw.tensor, 0, [[0, 128], [1, H]])))
            for i in range(4):
                g_init.dma(gp.dma_start(
                    xT[:, i * 4:(i + 1) * 4, :],
                    d_xT[i * 512:(i + 1) * 512, :].rearrange("(a p) t -> p a t", p=128)))
            for i in range(4):
                g_init.dma(gp.dma_start(
                    xtok[:, i * 2:(i + 1) * 2, :],
                    d_x[i * 256:(i + 1) * 256, :].rearrange("(a p) t -> p a t", p=128)))
            ck["gp_init_dma"] = g_init.v
            gp.memset(ones_cb[:], 1.0)
            gp.memset(ones_cf[:], 1.0)
            gp.memset(ones_rf[:], 1.0)
            gp.memset(cst[:, 0:1], EPS)
            gp.memset(cst[:, 1:2], 0.5e-8)
            gp.memset(cst[:, 2:3], -0.5e-8)
            gpc.inc(gp.memset(S_sb, 0.0))
            ck["gp_init"] = gpc.v

        blk.gpsimd(ph0_gp)

        def ph0_sync(sy):
            s_const.dma(sy.dma_start(id_sb[:], d_id))
            s_const.dma(sy.dma_start(mask_sb[:], d_mask))
            ck["sy_consts"] = s_const.v

        blk.sync(ph0_sync)

        def ph2_pe(te):
            te.wait_ge(g_init.h, ck["gp_init_dma"])
            for half in range(2):
                for hh in range(HT):
                    li = te.matmul(bank(half)[:, 0:512], wq_sb[:, hh, :],
                                   xT[:, hh, half * 512:(half + 1) * 512],
                                   start=(hh == 0), stop=(hh == HT - 1))
                pe.inc(li)
                ck[f"pe_q{half}"] = pe.v
            for half in range(2):
                for hh in range(HT):
                    li = te.matmul(bank(2 + half)[:, 0:512], wk_sb[:, hh, :],
                                   xT[:, hh, half * 512:(half + 1) * 512],
                                   start=(hh == 0), stop=(hh == HT - 1))
                pe.inc(li)
                ck[f"pe_k{half}"] = pe.v

        blk.tensor(ph2_pe)

        def ph2_sc(s):
            for half in range(2):
                s.wait_ge(pe.h, ck[f"pe_q{half}"])
                sc.inc(s.activation(siluq[:, half * 512:(half + 1) * 512],
                                    bank(half)[:, 0:512], AF.Silu))
            ck["sc_siluq"] = sc.v
            for half in range(2):
                s.wait_ge(pe.h, ck[f"pe_k{half}"])
                sc.inc(s.activation(ksig[:, half * 512:(half + 1) * 512],
                                    bank(2 + half)[:, 0:512], AF.Sigmoid))
            ck["sc_ksig"] = sc.v
            s.wait_ge(sc.h, sc.v)
            sc.inc(s.activation(E, ksig, AF.Sigmoid, scale=-1.0))
            s.wait_ge(sc.h, sc.v)
            sc.inc(s.activation(ccum, E, AF.Ln))
            ck["sc_sp"] = sc.v

        blk.scalar(ph2_sc)

        def ph2_ve(v):
            v.wait_ge(sc.h, ck["sc_sp"])
            for c in range(NCH):
                sl = slice(c * C, (c + 1) * C)
                li = v.tensor_tensor_scan(sp[:, sl], ccum[:, sl], ccum[:, sl], 0.0, ALU.add, ALU.bypass)
            ve.inc(li)
            ck["ve_scan"] = ve.v

        blk.vector(ph2_ve)

        def ph2_sc2(s):
            s.wait_ge(ve.h, ck["ve_scan"])
            sc.inc(s.activation(E, sp, AF.Exp, scale=1.0))
            sc.inc(s.activation(Einv, sp, AF.Exp, scale=-1.0))
            ck["sc_E"] = sc.v

        blk.scalar(ph2_sc2)

        def ph2_ve2(v):
            v.wait_ge(sc.h, ck["sc_E"])
            ve.inc(v.scalar_tensor_tensor(qp, siluq, SCALE, E, ALU.mult, ALU.mult))
            ve.inc(v.tensor_mul(kp, ksig, Einv))
            v.wait_ge(ve.h, ve.v)
            for c in range(NCH):
                sl = slice(c * C, (c + 1) * C)
                li = v.tensor_scalar(kpp[:, sl], kp[:, sl],
                                     E[:, c * C + C - 1:c * C + C], None, ALU.mult)
            ve.inc(li)
            ck["ve_prep"] = ve.v

        blk.vector(ph2_ve2)

        for pcn in range(NPC):
            def pre_gp(gp, pcn=pcn):
                if pcn >= 1:
                    gp.wait_ge(pe.h, ck[f"pe_pre_state{pcn - 1}"])
                g_pre.dma(gp.dma_start(
                    vpre[:, :, :],
                    d_xp[pcn * PC:(pcn + 1) * PC, :].rearrange("(a p) t -> p a t", p=128)))
                g_pre.dma(gp.dma_start(
                    xpreT[:, :, :],
                    d_xpT[:, pcn * PC:(pcn + 1) * PC].rearrange("(a p) t -> p a t", p=128)))
                ck[f"gp_pre{pcn}"] = g_pre.v

            blk.gpsimd(pre_gp)

            def pre_pe1(te, pcn=pcn):
                te.wait_ge(g_pre.h, ck[f"gp_pre{pcn}"])
                if pcn == 0:
                    te.wait_ge(g_init.h, ck["gp_init_dma"])
                if pcn == 0:
                    te.wait_ge(sc.h, ck["sc_siluq"])
                else:
                    te.wait_ge(sc.h, ck[f"sc_pre_ks{pcn - 1}"])
                for hh in range(HT):
                    li = te.matmul(bank(0)[:, 0:512], wk_sb[:, hh, :], xpreT[:, hh, :],
                                   start=(hh == 0), stop=(hh == HT - 1))
                pe.inc(li)
                ck[f"pe_pre_k{pcn}"] = pe.v

            blk.tensor(pre_pe1)

            def pre_sc1(s, pcn=pcn):
                s.wait_ge(pe.h, ck[f"pe_pre_k{pcn}"])
                sc.inc(s.activation(kspre, bank(0)[:, 0:512], AF.Sigmoid))
                ck[f"sc_pre_ks{pcn}"] = sc.v
                s.wait_ge(sc.h, sc.v)
                sc.inc(s.activation(kfac, kspre, AF.Sigmoid, scale=-1.0))
                s.wait_ge(sc.h, sc.v)
                sc.inc(s.activation(sppre, kfac, AF.Ln))
                ck[f"sc_pre_sp{pcn}"] = sc.v

            blk.scalar(pre_sc1)

            def pre_ve1(v, pcn=pcn):
                v.wait_ge(sc.h, ck[f"sc_pre_sp{pcn}"])
                ve.inc(v.tensor_tensor_scan(ccpre, sppre, sppre, 0.0, ALU.add, ALU.bypass))
                ck[f"ve_pre_scan{pcn}"] = ve.v

            blk.vector(pre_ve1)

            def pre_sc2a(s, pcn=pcn):
                s.wait_ge(ve.h, ck[f"ve_pre_scan{pcn}"])
                sc.inc(s.activation(ncol, ccpre[:, PC - 1:PC], AF.Copy))
                sc.inc(s.activation(dcol, ccpre[:, PC - 1:PC], AF.Exp))
                ck[f"sc_pre_nc{pcn}"] = sc.v

            blk.scalar(pre_sc2a)

            def pre_ping(v, pcn=pcn):
                v.wait_ge(sc.h, ck[f"sc_pre_nc{pcn}"])
                ve.inc(v.nop())
                ck[f"ve_pre_ping{pcn}"] = ve.v

            blk.vector(pre_ping)

            def pre_sc2b(s, pcn=pcn):
                s.wait_ge(ve.h, ck[f"ve_pre_ping{pcn}"])
                sc.inc(s.activation(kfac, ccpre, AF.Exp, scale=-1.0, bias=ncol))
                ck[f"sc_pre_fac{pcn}"] = sc.v

            blk.scalar(pre_sc2b)

            def pre_ve2(v, pcn=pcn):
                v.wait_ge(sc.h, ck[f"sc_pre_fac{pcn}"])
                if pcn >= 1:
                    v.wait_ge(pe.h, ck[f"pe_pre_kT{pcn - 1}"])
                ve.inc(v.tensor_mul(kppre, kspre, kfac))
                ck[f"ve_pre_kpp{pcn}"] = ve.v

            blk.vector(pre_ve2)

            def pre_pe2(te, pcn=pcn):
                if pcn == 0:
                    te.wait_ge(s_const.h, ck["sy_consts"])
                te.wait_ge(ve.h, ck[f"ve_pre_kpp{pcn}"])
                if pcn >= 1:
                    te.wait_ge(ve.h, ck[f"ve_pre_kT{pcn - 1}"])
                for sbk in range(4):
                    li = te.transpose(psbf[:, sbk * 128:(sbk + 1) * 128],
                                      kppre[:, sbk * 128:(sbk + 1) * 128], id_sb[:])
                pe.inc(li)
                ck[f"pe_pre_kT{pcn}"] = pe.v

            blk.tensor(pre_pe2)

            def pre_ve3(v, pcn=pcn):
                v.wait_ge(pe.h, ck[f"pe_pre_kT{pcn}"])
                if pcn >= 1:
                    v.wait_ge(pe.h, ck[f"pe_pre_state{pcn - 1}"])
                ve.inc(v.tensor_copy(rbig[:, 42176:42688], psbf[:, 0:512]))
                ck[f"ve_pre_kT{pcn}"] = ve.v

            blk.vector(pre_ve3)

            for q in range(4):
                def pre_pe3(te, pcn=pcn, q=q):
                    if q == 0:
                        te.wait_ge(ve.h, ck[f"ve_pre_kT{pcn}"])
                        if pcn == 0:
                            te.wait_ge(sc.h, ck["sc_ksig"])
                            te.wait_ge(gpc.h, ck["gp_init"])
                    bq = bank(2 + (q % 2))
                    if q >= 2:
                        te.wait_ge(ve.h, ck[f"ve_pre_stt{pcn}_{q - 2}"])
                    elif pcn >= 1:
                        te.wait_ge(ve.h, ck[f"ve_pre_stt{pcn - 1}_{q + 2}"])
                    for sbk in range(4):
                        li = te.matmul(bq[:, 0:512], kppTp[:, sbk, :],
                                       vpre[:, sbk, q * 512:(q + 1) * 512],
                                       start=(sbk == 0), stop=(sbk == 3))
                    pe.inc(li)
                    ck[f"pe_pre_st{pcn}_{q}"] = pe.v
                    if q == 3:
                        ck[f"pe_pre_state{pcn}"] = pe.v

                blk.tensor(pre_pe3)

                def pre_ve4(v, pcn=pcn, q=q):
                    v.wait_ge(pe.h, ck[f"pe_pre_st{pcn}_{q}"])
                    ve.inc(v.scalar_tensor_tensor(
                        S_sb[:, q * 512:(q + 1) * 512], S_sb[:, q * 512:(q + 1) * 512],
                        dcol, bank(2 + (q % 2))[:, 0:512], ALU.mult, ALU.add))
                    ck[f"ve_pre_stt{pcn}_{q}"] = ve.v

                blk.vector(pre_ve4)

        ck["ve_prefix_done"] = ve.v

        if _DEBUG_TAPS:
            def tap_pre(gp):
                gp.wait_ge(ve.h, ck["ve_prefix_done"])
                gp.wait_ge(ve.h, ck["ve_prep"])
                d_dbg.dma(gp.dma_start(dbg["S"], S_sb.rearrange("p (a t) -> p a t", a=1)[:, 0, :]))
                d_dbg.dma(gp.dma_start(dbg["ksig"], ksig))
                d_dbg.dma(gp.dma_start(dbg["siluq"], siluq))
                ck["dbg_pre"] = d_dbg.v
            blk.gpsimd(tap_pre)

        emit_gamma(48)

        for c in range(NCH):
            par = c % 2
            pr = c // 2
            cs = slice(c * C, (c + 1) * C)
            cend = E[:, c * C + C - 1:c * C + C]

            if par == 1:
                def gla_gp(gp, c=c):
                    vb = vbuf0 if (c // 2) % 2 == 0 else vbuf1
                    vs = g_vb0 if (c // 2) % 2 == 0 else g_vb1
                    if c >= 5:
                        gp.wait_ge(pe.h, ck[f"pe_oq3_{c - 4}"])
                    gp.wait_ge(g_init.h, ck["gp_init_dma"])
                    vs.dma(gp.dma_start(vb[0:64, :], xtok[64:128, c // 2, :]))
                    ck[f"gp_vb{c}"] = (vs.h, vs.v)

                blk.gpsimd(gla_gp)

            def gla_pe_a(te, c=c, cs=cs):
                if c == 0:
                    te.wait_ge(ve.h, ck["ve_prep"])
                    te.wait_ge(ve.h, ck["ve_prefix_done"])
                    te.wait_ge(s_const.h, ck["sy_consts"])
                    if _DEBUG_TAPS:
                        te.wait_ge(d_dbg.h, ck["dbg_pre"])
                if c >= 1:
                    te.wait_ge(ve.h, ck[f"ve_Pevac{c - 1}"])
                    te.wait_ge(ve.h, ck[f"ve_kTevac{c - 1}"])
                pe.inc(te.matmul(bank(6)[0:64, 0:64], kp[:, cs], qp[:, cs], start=True, stop=True))
                ck[f"pe_P{c}"] = pe.v
                pe.inc(te.transpose(psbf[0:64, 512:640], kpp[:, cs], id_sb[:]))
                ck[f"pe_kT{c}"] = pe.v

            blk.tensor(gla_pe_a)

            def gla_ve_a(v, c=c):
                v.wait_ge(pe.h, ck[f"pe_P{c}"])
                if c >= 1:
                    v.wait_ge(pe.h, ck[f"pe_oq3_{c - 1}"])
                ve.inc(v.tensor_mul(P_sb[0:64, :], bank(6)[0:64, 0:64], mask_sb[:]))
                ck[f"ve_Pevac{c}"] = ve.v
                v.wait_ge(pe.h, ck[f"pe_kT{c}"])
                ve.inc(v.tensor_copy(kppT_sb[0:64, :], psbf[0:64, 512:640]))
                ck[f"ve_kTevac{c}"] = ve.v

            blk.vector(gla_ve_a)

            for q in range(4):
                def gla_pe_b(te, c=c, par=par, pr=pr, cs=cs, q=q):
                    vb = ((vbuf0 if (c // 2) % 2 == 0 else vbuf1)[0:64, :] if par
                          else xtok[0:64, c // 2, :])
                    P_l = P_sb[0:64, :]
                    kT_l = kppT_sb[0:64, :]
                    tp = (0, 64) if par == 1 else None
                    orow = slice(64, 128) if par == 1 else slice(0, 64)
                    if q == 0:
                        te.wait_ge(ve.h, ck[f"ve_kTevac{c}"])
                        te.wait_ge(ve.h, ck[f"ve_Pevac{c}"])
                        if par == 1:
                            vsh, vsv = ck[f"gp_vb{c}"]
                            te.wait_ge(vsh, vsv)
                        if par == 0 and pr >= 1:
                            te.wait_ge(ve.h, ck[f"ve_oevac{pr - 1}"])
                    sbank = bank(4 + (q % 2))
                    if q >= 2:
                        te.wait_ge(ve.h, ck[f"ve_stt{c}_{q - 2}"])
                    elif c >= 1:
                        te.wait_ge(ve.h, ck[f"ve_stt{c - 1}_{q + 2}"])
                    te.matmul(sbank[:, 0:512], kT_l, vb[:, q * 512:(q + 1) * 512],
                              start=True, stop=True)
                    te.matmul(bank(q)[orow, 0:512], P_l, vb[:, q * 512:(q + 1) * 512],
                              start=True, stop=False, tile_position=tp)
                    pe.inc(te.matmul(bank(q)[orow, 0:512], qp[:, cs], S_sb[:, q * 512:(q + 1) * 512],
                                     start=False, stop=True, tile_position=tp))
                    ck[f"pe_oq{q}_{c}"] = pe.v
                    if q == 3:
                        ck[f"pe_oq3_{c}"] = pe.v

                blk.tensor(gla_pe_b)

                def gla_ve_b(v, c=c, cend=cend, q=q):
                    v.wait_ge(pe.h, ck[f"pe_oq{q}_{c}"])
                    ve.inc(v.scalar_tensor_tensor(
                        S_sb[:, q * 512:(q + 1) * 512], S_sb[:, q * 512:(q + 1) * 512],
                        cend, bank(4 + (q % 2))[:, 0:512], ALU.mult, ALU.add))
                    ck[f"ve_stt{c}_{q}"] = ve.v

                blk.vector(gla_ve_b)

            if par == 1:
                def gla_ve_ev(v, c=c, pr=pr):
                    v.wait_ge(pe.h, ck[f"pe_oq3_{c}"])
                    for q in range(4):
                        li = v.tensor_copy(o_tok[:, pr, q * 512:(q + 1) * 512], bank(q)[:, 0:512])
                    ve.inc(li)
                    ck[f"ve_oevac{pr}"] = ve.v

                blk.vector(gla_ve_ev)

        ck["gla_done_ve"] = ve.v

        if _DEBUG_TAPS:
            def tap_o(gp):
                gp.wait_ge(ve.h, ck["gla_done_ve"])
                for tt in range(TT):
                    d_dbg.dma(gp.dma_start(dbg["o"][tt * 128:(tt + 1) * 128, :], o_tok[:, tt, :]))
                ck["dbg_o"] = d_dbg.v
            blk.gpsimd(tap_o)

        def ph5_sc(s):
            for tt in range(TT):
                s.wait_ge(ve.h, ck[f"ve_oevac{tt}"])
                s.wait_ge(sc.h, sc.v)
                sc.inc(s.activation(junk_b[:, 0:2048], o_tok[:, tt, :], AF.Square,
                                    accum_out=oss[:, tt:tt + 1]))
            s.wait_ge(sc.h, sc.v)
            sc.inc(s.activation(lnt[:, 0:8], oss[:, 0:8], AF.Ln, scale=1.0 / H, bias=cst[:, 0:1]))
            s.wait_ge(sc.h, sc.v)
            sc.inc(s.activation(rstdo[:, 0:8], lnt[:, 0:8], AF.Exp, scale=-0.5))
            ck["sc_rstdo"] = sc.v

        blk.scalar(ph5_sc)

        ntr = 0
        for pr in range(TT):
            for vg in range(2):
                def ph5_pe(te, pr=pr, vg=vg, ntr=ntr):
                    te.wait_ge(ve.h, ck[f"ve_oevac{pr}"])
                    if ntr == 0:
                        te.wait_ge(ve.h, ck[f"ve_kTevac{NCH - 1}"])
                    else:
                        te.wait_ge(ve.h, ck[f"ve_oT{ntr - 1}"])
                    for j in range(8):
                        vt = vg * 8 + j
                        li = te.transpose(psbf[:, j * 128:(j + 1) * 128],
                                          o_tok[:, pr, vt * 128:(vt + 1) * 128], id_sb[:])
                    pe.inc(li)
                    ck[f"pe_oT{ntr}"] = pe.v

                blk.tensor(ph5_pe)

                def ph5_ve(v, pr=pr, vg=vg, ntr=ntr):
                    v.wait_ge(pe.h, ck[f"pe_oT{ntr}"])
                    for j in range(8):
                        vt = vg * 8 + j
                        li = v.tensor_copy(oTv[:, vt, pr * 128:(pr + 1) * 128],
                                           psbf[:, j * 128:(j + 1) * 128])
                    ve.inc(li)
                    ck[f"ve_oT{ntr}"] = ve.v

                blk.vector(ph5_ve)
                ntr += 1

        ck["ve_oT_done"] = ve.v

        gseq = 0
        p6_lastuse = {}
        for os_i in range(NOS):
            osl = slice(os_i * OS, (os_i + 1) * OS)
            wb, gb = (wogA, wigA) if os_i % 2 == 0 else (wogB, wigB)

            def ph6_gp(gp, os_i=os_i, wb=wb, gb=gb):
                ws = g_wogA if os_i % 2 == 0 else g_wogB
                if os_i == 1:
                    gp.wait_ge(ve.h, ck["gla_done_ve"])
                if os_i >= 2:
                    gp.wait_ge(pe.h, ck[f"pe_go{os_i - 2}_{TT - 1}"])
                ws.dma(gp.dma_start(
                    wb[:, :, :],
                    d_wog.rearrange("(a p) o -> p a o", p=128)[:, :, os_i * OS:(os_i + 1) * OS]))
                ws.dma(gp.dma_start(
                    gb[:, :, :],
                    d_wig.rearrange("(a p) o -> p a o", p=128)[:, :, os_i * OS:(os_i + 1) * OS]))
                ck[f"gp_wog{os_i}"] = (ws.h, ws.v)

            blk.gpsimd(ph6_gp)

            for tt in range(TT):
                def ph6_pe(te, os_i=os_i, tt=tt, wb=wb, gb=gb):
                    if tt == 0:
                        wsh, wsv = ck[f"gp_wog{os_i}"]
                        te.wait_ge(wsh, wsv)
                        if os_i == 0:
                            te.wait_ge(ve.h, ck["ve_oT_done"])
                    prev = p6_lastuse.get(tt % 6)
                    if prev is not None:
                        te.wait_ge(sc.h, ck[f"sc_sg{prev[0]}_{prev[1]}"])
                    p6_lastuse[tt % 6] = (os_i, tt)
                    bk = bank(tt % 6)
                    for vt in range(HT):
                        te.matmul(bk[:, 0:OS], oTv[:, vt, tt * 128:(tt + 1) * 128],
                                  wb[:, vt, :], start=(vt == 0), stop=False)
                        li = te.matmul(bk[:, 0:OS], xT[:, vt, tt * 128:(tt + 1) * 128],
                                       gb[:, vt, :], start=False, stop=(vt == HT - 1))
                    pe.inc(li)
                    ck[f"pe_go{os_i}_{tt}"] = pe.v

                blk.tensor(ph6_pe)

                def ph6_sc(s, os_i=os_i, tt=tt):
                    s.wait_ge(pe.h, ck[f"pe_go{os_i}_{tt}"])
                    cur = os_i * TT + tt
                    if cur >= 2:
                        po, pt = divmod(cur - 2, TT)
                        s.wait_ge(ve.h, ck[f"ve_ep{po}_{pt}"])
                    sgb = sg0 if tt % 2 == 0 else sg1
                    sc.inc(s.activation(sgb, bank(tt % 6)[:, 0:OS], AF.Silu))
                    ck[f"sc_sg{os_i}_{tt}"] = sc.v

                blk.scalar(ph6_sc)
                emit_gamma(3 if gseq % 2 == 0 else 4)
                gseq += 1

                def ph6_ve(v, os_i=os_i, tt=tt, osl=osl):
                    v.wait_ge(sc.h, ck[f"sc_sg{os_i}_{tt}"])
                    if _DEBUG_TAPS and os_i == 0 and tt == 0:
                        v.wait_ge(d_dbg.h, ck["dbg_o"])
                    sgb = sg0 if tt % 2 == 0 else sg1
                    v.wait_ge(ve.h, ve.v)
                    ve.inc(v.scalar_tensor_tensor(og6, o_tok[:, tt, osl], rstdo[:, tt:tt + 1],
                                                  gw_b[:, osl], ALU.mult, ALU.mult))
                    v.wait_ge(ve.h, ve.v)
                    ve.inc(v.scalar_tensor_tensor(
                        o_tok[:, tt, osl], sgb, 1.0, og6, ALU.mult, ALU.mult))
                    ck[f"ve_ep{os_i}_{tt}"] = ve.v

                blk.vector(ph6_ve)

        emit_gamma(NGALL)

        for tt in range(TT):
            def ph6b_gp(gp, tt=tt):
                xr = xres0 if tt % 2 == 0 else xres1
                xsem = g_xr0 if tt % 2 == 0 else g_xr1
                if tt < 2:
                    gp.wait_ge(ve.h, ck["ve_prep"])
                else:
                    gp.wait_ge(ve.h, ck[f"ve_z{tt - 2}"])
                xsem.dma(gp.dma_start(xr, d_x[tt * 128:(tt + 1) * 128, :]))
                ck[f"gp_xres{tt}"] = (xsem.h, xsem.v)

            blk.gpsimd(ph6b_gp)

            def ph6b_ve(v, tt=tt):
                xr = xres0 if tt % 2 == 0 else xres1
                xsh, xsv = ck[f"gp_xres{tt}"]
                v.wait_ge(xsh, xsv)
                v.wait_ge(ve.h, ck[f"ve_ep{NOS - 1}_{tt}"])
                ve.inc(v.scalar_tensor_tensor(
                    o_tok[:, tt, :], xr, 1.0, o_tok[:, tt, :], ALU.mult, ALU.add,
                    accum_out=sum_c[:, tt:tt + 1]))
                ck[f"ve_z{tt}"] = ve.v

            blk.vector(ph6b_ve)

            def ph6b_sc(s, tt=tt):
                s.wait_ge(ve.h, ck[f"ve_z{tt}"])
                s.wait_ge(sc.h, sc.v)
                sc.inc(s.activation(junk_b[:, 0:2048], o_tok[:, tt, :], AF.Square,
                                    accum_out=sq_c[:, tt:tt + 1]))
                ck[f"sc_zsq{tt}"] = sc.v

            blk.scalar(ph6b_sc)

        def ph6b_sc2a(s):
            s.wait_ge(sc.h, ck[f"sc_zsq{TT - 1}"])
            sc.inc(s.activation(mean_c[:, 0:TT], sum_c[:, 0:TT], AF.Copy, scale=1.0 / H))
            sc.inc(s.activation(e2_c[:, 0:TT], sq_c[:, 0:TT], AF.Copy, scale=1.0 / H))
            s.wait_ge(sc.h, sc.v)
            sc.inc(s.activation(m2_c[:, 0:TT], mean_c[:, 0:TT], AF.Square))
            s.wait_ge(sc.h, sc.v)
            sc.inc(s.activation(nm2_c[:, 0:TT], m2_c[:, 0:TT], AF.Copy, scale=-1.0, bias=EPS))
            ck["sc_ln1a"] = sc.v

        blk.scalar(ph6b_sc2a)

        def ph6b_ping(v):
            v.wait_ge(sc.h, ck["sc_ln1a"])
            ve.inc(v.nop())
            ck["ve_ln1ping"] = ve.v

        blk.vector(ph6b_ping)

        def ph6b_sc2b(s):
            s.wait_ge(ve.h, ck["ve_ln1ping"])
            for tt in range(TT):
                sc.inc(s.activation(lnt[:, tt:tt + 1], e2_c[:, tt:tt + 1], AF.Ln,
                                    bias=nm2_c[:, tt:tt + 1]))
            s.wait_ge(sc.h, sc.v)
            sc.inc(s.activation(r_c[:, 0:TT], lnt[:, 0:TT], AF.Exp, scale=-0.5))
            ck["sc_ln1"] = sc.v

        blk.scalar(ph6b_sc2b)

        def ph6b_ve2(v):
            v.wait_ge(sc.h, ck["sc_ln1"])
            for tt in range(TT):
                li = v.tensor_scalar(o_tok[:, tt, :], o_tok[:, tt, :],
                                     mean_c[:, tt:tt + 1], r_c[:, tt:tt + 1],
                                     ALU.subtract, ALU.mult)
            ve.inc(li)
            ck["ve_y1"] = ve.v

        blk.vector(ph6b_ve2)

        if _DEBUG_TAPS:
            def tap_y1(gp):
                gp.wait_ge(ve.h, ck["ve_y1"])
                for tt in range(TT):
                    d_dbg.dma(gp.dma_start(dbg["y1"][tt * 128:(tt + 1) * 128, :], o_tok[:, tt, :]))
                ck["dbg_y1"] = d_dbg.v
            blk.gpsimd(tap_y1)

        def gfin_pe(te):
            te.wait_ge(sc.h, ck[f"g_red{NGALL - 1}"])
            te.wait_ge(sc.h, ck[f"sc_sg{NOS - 1}_{TT - 1}"])
            pe.inc(te.matmul(bank(4)[0:1, 0:NGALL], ones_cf[:], gpart[:, 0:NGALL], start=True, stop=True))
            ck["pe_gsum"] = pe.v

        blk.tensor(gfin_pe)

        def gfin_ve(v):
            v.wait_ge(pe.h, ck["pe_gsum"])
            for m in range(3):
                a0, a1 = gcols[m]
                li = v.tensor_reduce(grow[0:1, m:m + 1], bank(4)[0:1, a0:a1], AX.X, ALU.add)
            ve.inc(li)
            ck["ve_gsum"] = ve.v

        blk.vector(gfin_ve)

        def gfin_sc(s):
            s.wait_ge(ve.h, ck["ve_gsum"])
            sc.inc(s.activation(throw[0:1, 0:3], grow[0:1, 0:3], AF.Copy,
                                scale=0.5 / QCNT, bias=0.5e-8))
            sc.inc(s.activation(nthrow[0:1, 0:3], grow[0:1, 0:3], AF.Copy,
                                scale=-0.5 / QCNT, bias=-0.5e-8))
            ck["sc_th"] = sc.v

        blk.scalar(gfin_sc)

        def gfin_pe2(te):
            te.wait_ge(sc.h, ck["sc_th"])
            te.matmul(bank(4)[:, 0:3], ones_rf[:], throw[0:1, 0:3], start=True, stop=True)
            pe.inc(te.matmul(bank(4)[:, 16:19], ones_rf[:], nthrow[0:1, 0:3], start=True, stop=True))
            ck["pe_thb"] = pe.v

        blk.tensor(gfin_pe2)

        def gfin_sc2(s):
            s.wait_ge(pe.h, ck["pe_thb"])
            sc.inc(s.copy(th_c[:, 0:3], bank(4)[:, 0:3]))
            sc.inc(s.copy(nth_c[:, 0:3], bank(4)[:, 16:19]))
            ck["sc_thc"] = sc.v

        blk.scalar(gfin_sc2)

        def gfin_ve2(v):
            v.wait_ge(sc.h, ck["sc_thc"])
            ve.inc(v.tensor_copy(og6[:, 0:1], ones_cb[:]))
            ck["ve_th"] = ve.v

        blk.vector(gfin_ve2)

        if _DEBUG_TAPS:
            blk.gpsimd(lambda gp: (
                gp.wait_ge(ve.h, ck["ve_th"]),
                d_dbg.dma(gp.dma_start(dbg["th"][:, 0:3], th_c[:, 0:3])),
                d_dbg.dma(gp.dma_start(dbg["th"][:, 4:7], nth_c[:, 0:3])),
                ck.__setitem__("dbg_th", d_dbg.v)))

        ntr = 0
        for pr in range(TT):
            for vg in range(2):
                def ph7_pe(te, pr=pr, vg=vg, ntr=ntr):
                    if ntr == 0:
                        te.wait_ge(ve.h, ck["ve_y1"])
                        te.wait_ge(ve.h, ck["ve_th"])
                        te.wait_ge(sc.h, ck["sc_ln1"])
                    else:
                        te.wait_ge(ve.h, ck[f"ve_y1T{ntr - 1}"])
                    for j in range(8):
                        vt = vg * 8 + j
                        li = te.transpose(psbf[:, j * 128:(j + 1) * 128],
                                          o_tok[:, pr, vt * 128:(vt + 1) * 128], id_sb[:])
                    pe.inc(li)
                    ck[f"pe_y1T{ntr}"] = pe.v

                blk.tensor(ph7_pe)

                def ph7_ve(v, pr=pr, vg=vg, ntr=ntr):
                    v.wait_ge(pe.h, ck[f"pe_y1T{ntr}"])
                    for j in range(8):
                        vt = vg * 8 + j
                        li = v.tensor_copy(y1T[:, vt, pr * 128:(pr + 1) * 128],
                                           psbf[:, j * 128:(j + 1) * 128])
                    ve.inc(li)
                    ck[f"ve_y1T{ntr}"] = ve.v

                blk.vector(ph7_ve)
                ntr += 1

        ck["ve_y1T_done"] = ve.v

        for ft in range(FT):
            wr1, wr2 = (w1rA, w2rA) if ft % 2 == 0 else (w1rB, w2rB)

            def ph8_sy(sy, ft=ft, wr1=wr1, wr2=wr2):
                wsem = s_w12A if ft % 2 == 0 else s_w12B
                if ft < 2:
                    sy.wait_ge(ve.h, ck["ve_y1T_done"])
                else:
                    sy.wait_ge(ve.h, ck[f"ve_q12_{ft - 2}"])
                wsem.dma(sy.dma_start(
                    wr1[:, :, :],
                    d_w1.rearrange("(a p) f -> p a f", p=128)[:, :, ft * 128:(ft + 1) * 128]))
                wsem.dma(sy.dma_start(
                    wr2[:, :, :],
                    d_w2.rearrange("(a p) f -> p a f", p=128)[:, :, ft * 128:(ft + 1) * 128]))
                ck[f"sy_w12_{ft}"] = (wsem.h, wsem.v)

            blk.sync(ph8_sy)

            def ph8_ve_q(v, ft=ft, wr1=wr1, wr2=wr2):
                wsh, wsv = ck[f"sy_w12_{ft}"]
                v.wait_ge(wsh, wsv)
                if ft == 0:
                    v.wait_ge(ve.h, ck["ve_th"])
                    v.wait_ge(ve.h, ck["ve_y1T_done"])
                    if _DEBUG_TAPS:
                        v.wait_ge(d_dbg.h, ck["dbg_th"])
                if ft >= 1:
                    v.wait_ge(pe.h, ck[f"pe_x12_{ft - 1}"])
                for wraw, wqd, m in ((wr1, wq1_sb, 0), (wr2, wq2_sb, 1)):
                    src = wraw.rearrange("p a t -> p (a t)")
                    dst = wqd.rearrange("p a t -> p (a t)")
                    v.wait_ge(ve.h, ve.v)
                    ve.inc(v.tensor_single_scalar(dst, src, nth_c[:, m:m + 1], ALU.is_le))
                    ve.inc(v.tensor_single_scalar(pq8, src, th_c[:, m:m + 1], ALU.is_ge))
                    v.wait_ge(ve.h, ve.v)
                    li = v.tensor_sub(dst, pq8, dst)
                    ve.inc(li)
                ck[f"ve_q12_{ft}"] = ve.v

            blk.vector(ph8_ve_q)

            def ph8_pe(te, ft=ft):
                te.wait_ge(ve.h, ck[f"ve_q12_{ft}"])
                bks = [(4 * ft + j) % 7 for j in range(4)]
                for j in range(4):
                    g = 4 * ft + j - 7
                    if g >= 0:
                        pft, pj = divmod(g, 4)
                        if pj < 2:
                            te.wait_ge(sc.h, ck[f"sc_sgf{pft}"])
                        else:
                            te.wait_ge(ve.h, ck[f"ve_h{pft}_{pj - 2}"])
                for half in range(2):
                    for hh in range(HT):
                        te.matmul(bank(bks[half])[:, 0:512], wq1_sb[:, hh, :],
                                  y1T[:, hh, half * 512:(half + 1) * 512],
                                  start=(hh == 0), stop=(hh == HT - 1))
                for half in range(2):
                    for hh in range(HT):
                        li = te.matmul(bank(bks[2 + half])[:, 0:512], wq2_sb[:, hh, :],
                                       y1T[:, hh, half * 512:(half + 1) * 512],
                                       start=(hh == 0), stop=(hh == HT - 1))
                pe.inc(li)
                ck[f"pe_x12_{ft}"] = pe.v

            blk.tensor(ph8_pe)

            def ph8_sc(s, ft=ft):
                s.wait_ge(pe.h, ck[f"pe_x12_{ft}"])
                if ft >= 1:
                    s.wait_ge(ve.h, ck[f"ve_h{ft - 1}_1"])
                bks = [(4 * ft + j) % 7 for j in range(4)]
                sc.inc(s.activation(sgf0, bank(bks[0])[:, 0:512], AF.Sigmoid))
                sc.inc(s.activation(sgf1, bank(bks[1])[:, 0:512], AF.Sigmoid))
                ck[f"sc_sgf{ft}"] = sc.v

            blk.scalar(ph8_sc)

            def ph8_ve_h(v, ft=ft):
                v.wait_ge(sc.h, ck[f"sc_sgf{ft}"])
                bks = [(4 * ft + j) % 7 for j in range(4)]
                ve.inc(v.tensor_mul(h_sb[:, ft, 0:512], sgf0, bank(bks[2])[:, 0:512]))
                ck[f"ve_h{ft}_0"] = ve.v
                ve.inc(v.tensor_mul(h_sb[:, ft, 512:1024], sgf1, bank(bks[3])[:, 0:512]))
                ck[f"ve_h{ft}_1"] = ve.v

            blk.vector(ph8_ve_h)

        if _DEBUG_TAPS:
            blk.gpsimd(lambda gp: (
                gp.wait_ge(ve.h, ck["ve_h0_1"]),
                d_dbg.dma(gp.dma_start(dbg["h0"], h_sb[:, 0, :]))))

        for dt in range(DT):
            wq3 = w3qA if dt % 2 == 0 else w3qB

            def ph9_sy(sy, dt=dt):
                if dt == 0:
                    sy.wait_ge(ve.h, ck[f"ve_q12_{FT - 1}"])
                else:
                    sy.wait_ge(ve.h, ck[f"ve_q3_{dt - 1}"])
                s_w3.dma(sy.dma_start(
                    w3r[:, :, :],
                    d_w3.rearrange("(a p) d -> p a d", p=128)[:, :, dt * 128:(dt + 1) * 128]))
                ck[f"sy_w3_{dt}"] = s_w3.v

            blk.sync(ph9_sy)

            def ph9_ve_q(v, dt=dt, wq3=wq3):
                v.wait_ge(s_w3.h, ck[f"sy_w3_{dt}"])
                if dt == 0:
                    v.wait_ge(ve.h, ck[f"ve_h{FT - 1}_1"])
                if dt >= 2:
                    v.wait_ge(pe.h, ck[f"pe_f3_{dt - 2}"])
                src = w3r.rearrange("p a t -> p (a t)")
                dst = wq3.rearrange("p a t -> p (a t)")
                for a, b_ in ((0, 2048), (2048, 4096), (4096, 5632)):
                    w = b_ - a
                    v.wait_ge(ve.h, ve.v)
                    ve.inc(v.tensor_single_scalar(dst[:, a:b_], src[:, a:b_], nth_c[:, 2:3], ALU.is_le))
                    ve.inc(v.tensor_single_scalar(pq9[:, 0:w], src[:, a:b_], th_c[:, 2:3], ALU.is_ge))
                    v.wait_ge(ve.h, ve.v)
                    ve.inc(v.tensor_sub(dst[:, a:b_], pq9[:, 0:w], dst[:, a:b_]))
                ck[f"ve_q3_{dt}"] = ve.v

            blk.vector(ph9_ve_q)

            def ph9_pe(te, dt=dt, wq3=wq3):
                te.wait_ge(ve.h, ck[f"ve_q3_{dt}"])
                if dt >= 2:
                    te.wait_ge(ve.h, ck[f"ve_z2_{dt - 2}"])
                b0 = bank(0 if dt % 2 == 0 else 2)
                b1 = bank(1 if dt % 2 == 0 else 3)
                for ftl in range(FT):
                    te.matmul(b0[:, 0:512], wq3[:, ftl, :], h_sb[:, ftl, 0:512],
                              start=(ftl == 0), stop=(ftl == FT - 1))
                    li = te.matmul(b1[:, 0:512], wq3[:, ftl, :], h_sb[:, ftl, 512:1024],
                                   start=(ftl == 0), stop=(ftl == FT - 1))
                pe.inc(li)
                ck[f"pe_f3_{dt}"] = pe.v

            blk.tensor(ph9_pe)

            def ph9_ve_z2(v, dt=dt):
                v.wait_ge(pe.h, ck[f"pe_f3_{dt}"])
                b0 = bank(0 if dt % 2 == 0 else 2)
                b1 = bank(1 if dt % 2 == 0 else 3)
                v.scalar_tensor_tensor(z2[:, dt, 0:512], y1T[:, dt, 0:512], 1.0,
                                       b0[:, 0:512], ALU.mult, ALU.add)
                ve.inc(v.scalar_tensor_tensor(z2[:, dt, 512:1024], y1T[:, dt, 512:1024], 1.0,
                                              b1[:, 0:512], ALU.mult, ALU.add))
                ck[f"ve_z2_{dt}"] = ve.v

            blk.vector(ph9_ve_z2)

            def ph9_sc(s, dt=dt):
                s.wait_ge(ve.h, ck[f"ve_z2_{dt}"])
                if dt >= 1:
                    s.wait_ge(pe.h, ck[f"pe_l2s_{dt - 1}"])
                s.wait_ge(sc.h, sc.v)
                sc.inc(s.activation(junk9[:, 0:1024], z2[:, dt, :], AF.Square))
                ck[f"sc_z2sq{dt}"] = sc.v

            blk.scalar(ph9_sc)

            def ph9_pe_st(te, dt=dt):
                te.wait_ge(sc.h, ck[f"sc_z2sq{dt}"])
                te.matmul(ps[0:1, 2048:2560], ones_cb[:], z2[:, dt, 0:512],
                          start=(dt == 0), stop=(dt == DT - 1))
                te.matmul(ps[0:1, 2560:3072], ones_cb[:], z2[:, dt, 512:1024],
                          start=(dt == 0), stop=(dt == DT - 1))
                te.matmul(ps[0:1, 3072:3584], ones_cb[:], junk9[:, 0:512],
                          start=(dt == 0), stop=(dt == DT - 1))
                pe.inc(te.matmul(ps[0:1, 3584:4096], ones_cb[:], junk9[:, 512:1024],
                                 start=(dt == 0), stop=(dt == DT - 1)))
                ck[f"pe_l2s_{dt}"] = pe.v

            blk.tensor(ph9_pe_st)

        def ph9b_ve(v):
            v.wait_ge(pe.h, ck[f"pe_l2s_{DT - 1}"])
            v.tensor_copy(sums_r[0:1, :], ps[0:1, 2048:3072])
            ve.inc(v.tensor_copy(ssq_r[0:1, :], ps[0:1, 3072:4096]))
            ck["ve_l2rows"] = ve.v

        blk.vector(ph9b_ve)

        def ph9b_sc(s):
            s.wait_ge(ve.h, ck["ve_l2rows"])
            sc.inc(s.activation(ln2a[0:1, :], sums_r[0:1, :], AF.Copy, scale=1.0 / H))
            sc.inc(s.activation(ln2b[0:1, :], ssq_r[0:1, :], AF.Copy, scale=1.0 / H))
            s.wait_ge(sc.h, sc.v)
            sc.inc(s.activation(sums_r[0:1, :], ln2a[0:1, :], AF.Square))
            ck["sc_l2a"] = sc.v

        blk.scalar(ph9b_sc)

        def ph9b_ve2(v):
            v.wait_ge(sc.h, ck["sc_l2a"])
            ve.inc(v.tensor_sub(ssq_r[0:1, :], ln2b[0:1, :], sums_r[0:1, :]))
            ck["ve_l2b"] = ve.v

        blk.vector(ph9b_ve2)

        def ph9b_sc2(s):
            s.wait_ge(ve.h, ck["ve_l2b"])
            sc.inc(s.activation(ln2b[0:1, :], ssq_r[0:1, :], AF.Ln, bias=cst[0:1, 0:1]))
            s.wait_ge(sc.h, sc.v)
            sc.inc(s.activation(ssq_r[0:1, :], ln2b[0:1, :], AF.Exp, scale=-0.5))
            ck["sc_l2r"] = sc.v

        blk.scalar(ph9b_sc2)

        def ph9b_pe(te):
            te.wait_ge(sc.h, ck["sc_l2r"])
            te.matmul(bank(4)[:, 0:512], ones_rf[:], ln2a[0:1, 0:512], start=True, stop=True)
            te.matmul(bank(5)[:, 0:512], ones_rf[:], ln2a[0:1, 512:1024], start=True, stop=True)
            te.matmul(bank(6)[:, 0:512], ones_rf[:], ssq_r[0:1, 0:512], start=True, stop=True)
            pe.inc(te.matmul(bank(0)[:, 0:512], ones_rf[:], ssq_r[0:1, 512:1024], start=True, stop=True))
            ck["pe_l2b"] = pe.v

        blk.tensor(ph9b_pe)

        def ph9b_ve3(v):
            v.wait_ge(pe.h, ck["pe_l2b"])
            v.tensor_copy(mb_sb[:, 0:512], bank(4)[:, 0:512])
            v.tensor_copy(mb_sb[:, 512:1024], bank(5)[:, 0:512])
            v.tensor_copy(rb_sb[:, 0:512], bank(6)[:, 0:512])
            ve.inc(v.tensor_copy(rb_sb[:, 512:1024], bank(0)[:, 0:512]))
            ck["ve_l2bc"] = ve.v

        blk.vector(ph9b_ve3)

        if _DEBUG_TAPS:
            blk.gpsimd(lambda gp: (
                gp.wait_ge(ve.h, ck["ve_l2bc"]),
                d_dbg.dma(gp.dma_start(dbg["z2"], z2[:, 0, :]))))

        for dt in range(DT):
            def ph9c_ve(v, dt=dt):
                if dt == 0:
                    v.wait_ge(ve.h, ck["ve_l2bc"])
                else:
                    v.wait_ge(o_dma.h, ck[f"od_{dt - 1}"])
                ve.inc(v.scalar_tensor_tensor(y2t, z2[:, dt, :], 1.0, mb_sb, ALU.mult, ALU.subtract))
                v.wait_ge(ve.h, ve.v)
                ve.inc(v.tensor_mul(y2t, y2t, rb_sb))
                ck[f"ve_y2_{dt}"] = ve.v

            blk.vector(ph9c_ve)

            def ph9c_sy(sy, dt=dt):
                sy.wait_ge(ve.h, ck[f"ve_y2_{dt}"])
                o_dma.dma(sy.dma_start(d_y[dt * 128:(dt + 1) * 128, :], y2t))
                ck[f"od_{dt}"] = o_dma.v

            blk.sync(ph9c_sy)

        def fin_sync(sy):
            sy.wait_ge(o_dma.h, o_dma.v)

        blk.sync(fin_sync)

        def fin_gp(gp):
            gp.wait_ge(o_dma.h, o_dma.v)
            if _DEBUG_TAPS:
                gp.wait_ge(d_dbg.h, d_dbg.v)

        blk.gpsimd(fin_gp)

    return nc


_CACHED = None


def kernel(L0_attn_hidden_states, L0_attn_Wq, L0_attn_Wk, L0_attn_Wog,
           L0_attn_Wig, L0_attn_g_weight, L0_ffn_w1, L0_ffn_w2, L0_ffn_w3):
    global _CACHED
    x = np.ascontiguousarray(L0_attn_hidden_states, dtype=np.float32)
    wqT = np.ascontiguousarray(np.asarray(L0_attn_Wq, dtype=np.float32).T)
    wkT = np.ascontiguousarray(np.asarray(L0_attn_Wk, dtype=np.float32).T)
    wogT = np.ascontiguousarray(np.asarray(L0_attn_Wog, dtype=np.float32).T)
    wigT = np.ascontiguousarray(np.asarray(L0_attn_Wig, dtype=np.float32).T)
    w1T = np.ascontiguousarray(np.asarray(L0_ffn_w1, dtype=np.float32).T)
    w2T = np.ascontiguousarray(np.asarray(L0_ffn_w2, dtype=np.float32).T)
    w3T = np.ascontiguousarray(np.asarray(L0_ffn_w3, dtype=np.float32).T)
    gw = np.ascontiguousarray(np.asarray(L0_attn_g_weight, dtype=np.float32))
    identb = np.eye(128, dtype=bfdt)
    masksb = np.triu(np.ones((64, 64), dtype=np.float32))

    in_maps = []
    for core in range(8):
        b, s = divmod(core, 4)
        t0 = s * T
        xs = np.ascontiguousarray(x[b, t0:t0 + T, :])
        xsT = np.ascontiguousarray(xs.T)
        xp = np.zeros((PRE, H), dtype=np.float32)
        if t0 > 0:
            xp[PRE - t0:] = x[b, 0:t0, :]
        xpT = np.ascontiguousarray(xp.T)
        in_maps.append({
            "x_seg": xs, "x_segT": xsT, "x_pre": xp, "x_preT": xpT,
            "wqT": wqT, "wkT": wkT, "wogT": wogT, "wigT": wigT,
            "w1T": w1T, "w2T": w2T, "w3T": w3T, "g_w": gw,
            "identb": identb, "masksb": masksb,
        })

    if _CACHED is None:
        _CACHED = _build()
    res = run_bass_kernel_spmd(_CACHED, in_maps, core_ids=list(range(8)))
    out = np.empty((B, L, H), dtype=np.float32)
    for core in range(8):
        b, s = divmod(core, 4)
        out[b, s * T:(s + 1) * T, :] = np.asarray(res.results[core]["y_out"]).T
    return out


# version 03
# speedup vs baseline: 1.0609x; 1.0609x over previous
import numpy as np
import ml_dtypes
from contextlib import ExitStack

import concourse.bass as bass
import concourse.mybir as mybir
from concourse.bass_utils import run_bass_kernel_spmd

F32 = mybir.dt.float32
BF16 = mybir.dt.bfloat16
AF = mybir.ActivationFunctionType
ALU = mybir.AluOpType
AX = mybir.AxisListType
bfdt = ml_dtypes.bfloat16

B, L, H, KD, FF = 2, 4096, 2048, 128, 5632
T = 1024
PRE = 3072
PC = 512
NPC = PRE // PC
C = 64
NCH = T // C
HT = H // 128
TT = T // 128
FT = FF // 128
DT = H // 128
OS = 256
NOS = H // OS
SCALE = float(KD) ** -0.5
EPS = 1e-5
QCNT = float(FF * H)
GT = 1024

_DEBUG_TAPS = False


class Sem:
    def __init__(self, ctx, nc, name):
        self.h = ctx.enter_context(nc.semaphore(name))
        self.v = 0

    def inc(self, inst, n=1):
        inst.then_inc(self.h, n)
        self.v += n
        return self.v

    def dma(self, inst):
        return self.inc(inst, 16)


def _build():
    nc = bass.Bass("TRN2", target_bir_lowering=False)

    d_x = nc.dram_tensor("x_seg", [T, H], F32, kind="ExternalInput").ap()
    d_xT = nc.dram_tensor("x_segT", [H, T], F32, kind="ExternalInput").ap()
    d_xp = nc.dram_tensor("x_pre", [PRE, H], F32, kind="ExternalInput").ap()
    d_xpT = nc.dram_tensor("x_preT", [H, PRE], F32, kind="ExternalInput").ap()
    d_wq = nc.dram_tensor("wqT", [H, KD], F32, kind="ExternalInput").ap()
    d_wk = nc.dram_tensor("wkT", [H, KD], F32, kind="ExternalInput").ap()
    d_wog = nc.dram_tensor("wogT", [H, H], F32, kind="ExternalInput").ap()
    d_wig = nc.dram_tensor("wigT", [H, H], F32, kind="ExternalInput").ap()
    d_w1 = nc.dram_tensor("w1T", [H, FF], F32, kind="ExternalInput").ap()
    d_w2 = nc.dram_tensor("w2T", [H, FF], F32, kind="ExternalInput").ap()
    d_w3 = nc.dram_tensor("w3T", [FF, H], F32, kind="ExternalInput").ap()
    d_gw = nc.dram_tensor("g_w", [H], F32, kind="ExternalInput").ap()
    d_id = nc.dram_tensor("identb", [128, 128], BF16, kind="ExternalInput").ap()
    d_mask = nc.dram_tensor("masksb", [64, 64], F32, kind="ExternalInput").ap()
    d_y = nc.dram_tensor("y_out", [H, T], F32, kind="ExternalOutput").ap()

    dbg = {}
    if _DEBUG_TAPS:
        dbg["S"] = nc.dram_tensor("dbg_S", [KD, H], F32, kind="ExternalOutput").ap()
        dbg["o"] = nc.dram_tensor("dbg_o", [T, H], F32, kind="ExternalOutput").ap()
        dbg["y1"] = nc.dram_tensor("dbg_y1", [T, H], F32, kind="ExternalOutput").ap()
        dbg["th"] = nc.dram_tensor("dbg_th", [128, 8], F32, kind="ExternalOutput").ap()
        dbg["ksig"] = nc.dram_tensor("dbg_ksig", [KD, T], F32, kind="ExternalOutput").ap()
        dbg["siluq"] = nc.dram_tensor("dbg_siluq", [KD, T], F32, kind="ExternalOutput").ap()
        dbg["h0"] = nc.dram_tensor("dbg_h0", [128, T], F32, kind="ExternalOutput").ap()
        dbg["z2"] = nc.dram_tensor("dbg_z2", [128, T], F32, kind="ExternalOutput").ap()

    ctx = ExitStack()
    with ctx:
        def sbt(name, shape, dt):
            return ctx.enter_context(nc.sbuf_tensor(name, shape, dt))

        wq_sb = sbt("wq_sb", [128, HT, 128], BF16)
        wk_sb = sbt("wk_sb", [128, HT, 128], BF16)
        gw_b = sbt("gw_b", [128, H], BF16)
        id_sb = sbt("id_sb", [128, 128], BF16)
        mask_sb = sbt("mask_sb", [64, 64], F32)
        ones_cb = sbt("ones_cb", [128, 1], BF16)
        ones_cf = sbt("ones_cf", [128, 1], F32)
        ones_rf = sbt("ones_rf", [1, 128], F32)
        cst = sbt("cst", [128, 3], F32)
        nc.const_aps.register(F32, EPS, cst[:, 0:1])
        nc.const_aps.register(F32, 0.5e-8, cst[:, 1:2])
        nc.const_aps.register(F32, -0.5e-8, cst[:, 2:3])

        rbig = sbt("rbig", [128, 45056], BF16)
        xT = rbig[:, 0:16384].rearrange("p (a t) -> p a t", a=HT)
        xtok = rbig[:, 16384:32768].rearrange("p (a t) -> p a t", a=TT)
        oTv = rbig[:, 16384:32768].rearrange("p (a t) -> p a t", a=HT)
        qp = rbig[:, 32768:33792]
        kp = rbig[:, 33792:34816]
        kpp = rbig[:, 34816:35840]
        S_sb = rbig[:, 35840:37888]
        vbuf0 = rbig[:, 37888:39936]
        vbuf1 = rbig[:, 39936:41984]
        kppT_sb = rbig[:, 41984:42112]
        P_sb = rbig[:, 42112:42176]
        kppTp = rbig[:, 42176:42688].rearrange("p (a t) -> p a t", a=4)
        kppre = rbig[:, 42688:43200]
        wogB = rbig[:, 32768:36864].rearrange("p (a t) -> p a t", a=HT)
        wigB = rbig[:, 36864:40960].rearrange("p (a t) -> p a t", a=HT)
        h_sb = rbig[:, 0:45056].rearrange("p (a t) -> p a t", a=FT)

        rb2 = sbt("rb2", [128, 16384], BF16)
        o_tok = rb2[:, 0:16384].rearrange("p (a t) -> p a t", a=TT)
        vpreA = rb2[:, 0:8192].rearrange("p (a t) -> p a t", a=4)
        xpreTA = rb2[:, 8192:16384].rearrange("p (a t) -> p a t", a=HT)
        wq1A = rb2[:, 0:2048].rearrange("p (a t) -> p a t", a=HT)
        wq2A = rb2[:, 2048:4096].rearrange("p (a t) -> p a t", a=HT)
        wq1B = rb2[:, 11264:13312].rearrange("p (a t) -> p a t", a=HT)
        wq2B = rb2[:, 13312:15360].rearrange("p (a t) -> p a t", a=HT)
        pq8 = rb2[:, 4096:6144]
        w3qA = rb2[:, 4096:9728].rearrange("p (a t) -> p a t", a=FT)
        w3qB = rb2[:, 9728:15360].rearrange("p (a t) -> p a t", a=FT)
        pq9 = rb2[:, 0:2048]
        junk9 = rb2[:, 15360:16384]

        rh = sbt("rh", [128, 16384], BF16)
        y1T = rh[:, 0:16384].rearrange("p (a t) -> p a t", a=HT)
        z2 = y1T
        junk_b = rh[:, 0:2048]
        vpreB = rh[:, 0:8192].rearrange("p (a t) -> p a t", a=4)
        xpreTB = rh[:, 8192:16384].rearrange("p (a t) -> p a t", a=HT)
        og6 = rh[:, 2048:2304]
        wogA = rh[:, 4096:8192].rearrange("p (a t) -> p a t", a=HT)
        wigA = rh[:, 8192:12288].rearrange("p (a t) -> p a t", a=HT)

        rf = sbt("rf", [128, 10752], F32)
        siluq = rf[:, 0:1024]
        ksig = rf[:, 1024:2048]
        sp = rf[:, 2048:3072]
        ccum = rf[:, 3072:4096]
        E = rf[:, 4096:5120]
        Einv = rf[:, 5120:6144]
        kspre = rf[:, 6144:6656]
        sppre = rf[:, 6656:7168]
        ccpre = rf[:, 7168:7680]
        kfac = rf[:, 7680:8192]
        gam0 = rf[:, 8192:9216]
        gam1 = rf[:, 9216:10240]
        stats = rf[:, 10240:10752]
        gpart = stats[:, 0:288]
        th_c = stats[:, 288:292]
        nth_c = stats[:, 292:296]
        oss = stats[:, 296:304]
        rstdo = stats[:, 304:312]
        lnt = stats[:, 312:320]
        sum_c = stats[:, 320:328]
        sq_c = stats[:, 328:336]
        mean_c = stats[:, 336:344]
        e2_c = stats[:, 344:352]
        m2_c = stats[:, 352:360]
        nm2_c = stats[:, 360:368]
        r_c = stats[:, 368:376]
        dcol = stats[:, 376:377]
        ncol = stats[:, 377:378]
        grow = stats[:, 384:388]
        throw = stats[:, 388:392]
        nthrow = stats[:, 392:396]
        xres0 = rf[:, 0:2048]
        xres1 = rf[:, 2048:4096]
        sg0 = rf[:, 4096:4352]
        sg1 = rf[:, 4352:4608]
        w1rA = rf[:, 0:2048].rearrange("p (a t) -> p a t", a=HT)
        w2rA = rf[:, 2048:4096].rearrange("p (a t) -> p a t", a=HT)
        w1rB = rf[:, 4096:6144].rearrange("p (a t) -> p a t", a=HT)
        w2rB = rf[:, 6144:8192].rearrange("p (a t) -> p a t", a=HT)
        sgf0 = rf[:, 8192:8704]
        sgf1 = rf[:, 8704:9216]
        w3r = rf[:, 0:5632].rearrange("p (a t) -> p a t", a=FT)
        mb_sb = rf[:, 5632:6656]
        rb_sb = rf[:, 6656:7680]
        y2t = rf[:, 7680:8704]
        sums_r = rf[:, 2048:3072]
        ssq_r = rf[:, 3072:4096]
        ln2a = rf[:, 0:1024]
        ln2b = rf[:, 1024:2048]

        ps = ctx.enter_context(nc.psum_tensor("ps", [128, 4096], F32))

        def bank(i):
            return ps[:, i * 512:(i + 1) * 512]

        psbf = ps[:, 3584:4096].bitcast(BF16)

        g_init = Sem(ctx, nc, "g_init")
        g_pre = Sem(ctx, nc, "g_pre")
        g_preB = Sem(ctx, nc, "g_preB")
        g_vb0 = Sem(ctx, nc, "g_vb0")
        g_vb1 = Sem(ctx, nc, "g_vb1")
        g_wogA = Sem(ctx, nc, "g_wogA")
        g_wogB = Sem(ctx, nc, "g_wogB")
        g_xr0 = Sem(ctx, nc, "g_xr0")
        g_xr1 = Sem(ctx, nc, "g_xr1")
        s_const = Sem(ctx, nc, "s_const")
        s_gam0 = Sem(ctx, nc, "s_gam0")
        s_gam1 = Sem(ctx, nc, "s_gam1")
        s_w12A = Sem(ctx, nc, "s_w12A")
        s_w12B = Sem(ctx, nc, "s_w12B")
        s_w3 = Sem(ctx, nc, "s_w3")
        o_dma = Sem(ctx, nc, "o_dma")
        d_dbg = Sem(ctx, nc, "d_dbg")
        pe = Sem(ctx, nc, "pe")
        ve = Sem(ctx, nc, "ve")
        sc = Sem(ctx, nc, "sc")
        gpc = Sem(ctx, nc, "gpc")
        ck = {}

        blk = ctx.enter_context(nc.Block())

        gjobs = []
        gcols = []
        for m, srcd in enumerate((d_w1, d_w2, d_w3)):
            r = srcd.rearrange("(a p) f -> p a f", p=128)
            fdim = r.shape[2]
            na = r.shape[1]
            c0 = len(gjobs)
            for a in range(na):
                off = 0
                while off < fdim:
                    w = min(GT, fdim - off)
                    gjobs.append((m, r[:, a, off:off + w], w))
                    off += w
            gcols.append((c0, len(gjobs)))
        NGALL = len(gjobs)
        assert NGALL <= 288
        gamma_n = [0]

        def emit_gamma(k):
            for _ in range(k):
                n = gamma_n[0]
                if n >= NGALL:
                    return
                m, gsrc, w = gjobs[n]
                buf = (gam0 if n % 2 == 0 else gam1)[:, 0:w]
                gsem = s_gam0 if n % 2 == 0 else s_gam1
                col = gpart[:, n:n + 1]

                def gdma(sy, n=n, gsrc=gsrc, buf=buf, gsem=gsem):
                    if n >= 2:
                        sy.wait_ge(ve.h, ck[f"g_red{n - 2}"])
                    gsem.dma(sy.dma_start(buf, gsrc))
                    ck[f"g_ld{n}"] = gsem.v

                blk.sync(gdma)

                def gred(v, n=n, buf=buf, col=col, w=w, gsem=gsem):
                    v.wait_ge(gsem.h, ck[f"g_ld{n}"])
                    ve.inc(v.tensor_reduce(col, buf, AX.X, ALU.add, apply_absolute_value=True))
                    ck[f"g_red{n}"] = ve.v

                blk.vector(gred)
                gamma_n[0] += 1

        def ph0_gp(gp):
            g_init.dma(gp.dma_start(wq_sb[:], d_wq.rearrange("(a p) k -> p a k", p=128)))
            g_init.dma(gp.dma_start(wk_sb[:], d_wk.rearrange("(a p) k -> p a k", p=128)))
            g_init.dma(gp.dma_start(gw_b[:], bass.AP(d_gw.tensor, 0, [[0, 128], [1, H]])))
            for i in range(4):
                g_init.dma(gp.dma_start(
                    xT[:, i * 4:(i + 1) * 4, :],
                    d_xT[i * 512:(i + 1) * 512, :].rearrange("(a p) t -> p a t", p=128)))
            for i in range(4):
                g_init.dma(gp.dma_start(
                    xtok[:, i * 2:(i + 1) * 2, :],
                    d_x[i * 256:(i + 1) * 256, :].rearrange("(a p) t -> p a t", p=128)))
            ck["gp_init_dma"] = g_init.v
            gp.memset(ones_cb[:], 1.0)
            gp.memset(ones_cf[:], 1.0)
            gp.memset(ones_rf[:], 1.0)
            gp.memset(cst[:, 0:1], EPS)
            gp.memset(cst[:, 1:2], 0.5e-8)
            gp.memset(cst[:, 2:3], -0.5e-8)
            gpc.inc(gp.memset(S_sb, 0.0))
            ck["gp_init"] = gpc.v

        blk.gpsimd(ph0_gp)

        def ph0_sync(sy):
            s_const.dma(sy.dma_start(id_sb[:], d_id))
            s_const.dma(sy.dma_start(mask_sb[:], d_mask))
            ck["sy_consts"] = s_const.v

        blk.sync(ph0_sync)

        def ph2_pe(te):
            te.wait_ge(g_init.h, ck["gp_init_dma"])
            for half in range(2):
                for hh in range(HT):
                    li = te.matmul(bank(half)[:, 0:512], wq_sb[:, hh, :],
                                   xT[:, hh, half * 512:(half + 1) * 512],
                                   start=(hh == 0), stop=(hh == HT - 1))
                pe.inc(li)
                ck[f"pe_q{half}"] = pe.v
            for half in range(2):
                for hh in range(HT):
                    li = te.matmul(bank(2 + half)[:, 0:512], wk_sb[:, hh, :],
                                   xT[:, hh, half * 512:(half + 1) * 512],
                                   start=(hh == 0), stop=(hh == HT - 1))
                pe.inc(li)
                ck[f"pe_k{half}"] = pe.v

        blk.tensor(ph2_pe)

        def ph2_sc(s):
            for half in range(2):
                s.wait_ge(pe.h, ck[f"pe_q{half}"])
                sc.inc(s.activation(siluq[:, half * 512:(half + 1) * 512],
                                    bank(half)[:, 0:512], AF.Silu))
            ck["sc_siluq"] = sc.v
            for half in range(2):
                s.wait_ge(pe.h, ck[f"pe_k{half}"])
                sc.inc(s.activation(ksig[:, half * 512:(half + 1) * 512],
                                    bank(2 + half)[:, 0:512], AF.Sigmoid))
            ck["sc_ksig"] = sc.v
            s.wait_ge(sc.h, sc.v)
            sc.inc(s.activation(E, ksig, AF.Sigmoid, scale=-1.0))
            s.wait_ge(sc.h, sc.v)
            sc.inc(s.activation(ccum, E, AF.Ln))
            ck["sc_sp"] = sc.v

        blk.scalar(ph2_sc)

        def ph2_ve(v):
            v.wait_ge(sc.h, ck["sc_sp"])
            for c in range(NCH):
                sl = slice(c * C, (c + 1) * C)
                li = v.tensor_tensor_scan(sp[:, sl], ccum[:, sl], ccum[:, sl], 0.0, ALU.add, ALU.bypass)
            ve.inc(li)
            ck["ve_scan"] = ve.v

        blk.vector(ph2_ve)

        def ph2_sc2(s):
            s.wait_ge(ve.h, ck["ve_scan"])
            sc.inc(s.activation(E, sp, AF.Exp, scale=1.0))
            sc.inc(s.activation(Einv, sp, AF.Exp, scale=-1.0))
            ck["sc_E"] = sc.v

        blk.scalar(ph2_sc2)

        def ph2_ve2(v):
            v.wait_ge(sc.h, ck["sc_E"])
            ve.inc(v.scalar_tensor_tensor(qp, siluq, SCALE, E, ALU.mult, ALU.mult))
            ve.inc(v.tensor_mul(kp, ksig, Einv))
            v.wait_ge(ve.h, ve.v)
            for c in range(NCH):
                sl = slice(c * C, (c + 1) * C)
                li = v.tensor_scalar(kpp[:, sl], kp[:, sl],
                                     E[:, c * C + C - 1:c * C + C], None, ALU.mult)
            ve.inc(li)
            ck["ve_prep"] = ve.v

        blk.vector(ph2_ve2)

        for pcn in range(NPC):
            def pre_gp(gp, pcn=pcn):
                vpre = vpreA if pcn % 2 == 0 else vpreB
                xpreT = xpreTA if pcn % 2 == 0 else xpreTB
                psem = g_pre if pcn % 2 == 0 else g_preB
                if pcn >= 2:
                    gp.wait_ge(pe.h, ck[f"pe_pre_state{pcn - 2}"])
                psem.dma(gp.dma_start(
                    vpre[:, :, :],
                    d_xp[pcn * PC:(pcn + 1) * PC, :].rearrange("(a p) t -> p a t", p=128)))
                psem.dma(gp.dma_start(
                    xpreT[:, :, :],
                    d_xpT[:, pcn * PC:(pcn + 1) * PC].rearrange("(a p) t -> p a t", p=128)))
                ck[f"gp_pre{pcn}"] = (psem.h, psem.v)

            blk.gpsimd(pre_gp)

            def pre_pe1(te, pcn=pcn):
                xpreT = xpreTA if pcn % 2 == 0 else xpreTB
                psh, psv = ck[f"gp_pre{pcn}"]
                te.wait_ge(psh, psv)
                if pcn == 0:
                    te.wait_ge(g_init.h, ck["gp_init_dma"])
                if pcn == 0:
                    te.wait_ge(sc.h, ck["sc_siluq"])
                else:
                    te.wait_ge(sc.h, ck[f"sc_pre_ks{pcn - 1}"])
                for hh in range(HT):
                    li = te.matmul(bank(0)[:, 0:512], wk_sb[:, hh, :], xpreT[:, hh, :],
                                   start=(hh == 0), stop=(hh == HT - 1))
                pe.inc(li)
                ck[f"pe_pre_k{pcn}"] = pe.v

            blk.tensor(pre_pe1)

            def pre_sc1(s, pcn=pcn):
                s.wait_ge(pe.h, ck[f"pe_pre_k{pcn}"])
                sc.inc(s.activation(kspre, bank(0)[:, 0:512], AF.Sigmoid))
                ck[f"sc_pre_ks{pcn}"] = sc.v
                s.wait_ge(sc.h, sc.v)
                sc.inc(s.activation(kfac, kspre, AF.Sigmoid, scale=-1.0))
                s.wait_ge(sc.h, sc.v)
                sc.inc(s.activation(sppre, kfac, AF.Ln))
                ck[f"sc_pre_sp{pcn}"] = sc.v

            blk.scalar(pre_sc1)

            def pre_ve1(v, pcn=pcn):
                v.wait_ge(sc.h, ck[f"sc_pre_sp{pcn}"])
                ve.inc(v.tensor_tensor_scan(ccpre, sppre, sppre, 0.0, ALU.add, ALU.bypass))
                ck[f"ve_pre_scan{pcn}"] = ve.v

            blk.vector(pre_ve1)

            def pre_sc2a(s, pcn=pcn):
                s.wait_ge(ve.h, ck[f"ve_pre_scan{pcn}"])
                sc.inc(s.activation(ncol, ccpre[:, PC - 1:PC], AF.Copy))
                sc.inc(s.activation(dcol, ccpre[:, PC - 1:PC], AF.Exp))
                ck[f"sc_pre_nc{pcn}"] = sc.v

            blk.scalar(pre_sc2a)

            def pre_ping(v, pcn=pcn):
                v.wait_ge(sc.h, ck[f"sc_pre_nc{pcn}"])
                ve.inc(v.nop())
                ck[f"ve_pre_ping{pcn}"] = ve.v

            blk.vector(pre_ping)

            def pre_sc2b(s, pcn=pcn):
                s.wait_ge(ve.h, ck[f"ve_pre_ping{pcn}"])
                sc.inc(s.activation(kfac, ccpre, AF.Exp, scale=-1.0, bias=ncol))
                ck[f"sc_pre_fac{pcn}"] = sc.v

            blk.scalar(pre_sc2b)

            def pre_ve2(v, pcn=pcn):
                v.wait_ge(sc.h, ck[f"sc_pre_fac{pcn}"])
                if pcn >= 1:
                    v.wait_ge(pe.h, ck[f"pe_pre_kT{pcn - 1}"])
                ve.inc(v.tensor_mul(kppre, kspre, kfac))
                ck[f"ve_pre_kpp{pcn}"] = ve.v

            blk.vector(pre_ve2)

            def pre_pe2(te, pcn=pcn):
                if pcn == 0:
                    te.wait_ge(s_const.h, ck["sy_consts"])
                te.wait_ge(ve.h, ck[f"ve_pre_kpp{pcn}"])
                if pcn >= 1:
                    te.wait_ge(ve.h, ck[f"ve_pre_kT{pcn - 1}"])
                for sbk in range(4):
                    li = te.transpose(psbf[:, sbk * 128:(sbk + 1) * 128],
                                      kppre[:, sbk * 128:(sbk + 1) * 128], id_sb[:])
                pe.inc(li)
                ck[f"pe_pre_kT{pcn}"] = pe.v

            blk.tensor(pre_pe2)

            def pre_ve3(v, pcn=pcn):
                v.wait_ge(pe.h, ck[f"pe_pre_kT{pcn}"])
                if pcn >= 1:
                    v.wait_ge(pe.h, ck[f"pe_pre_state{pcn - 1}"])
                ve.inc(v.tensor_copy(rbig[:, 42176:42688], psbf[:, 0:512]))
                ck[f"ve_pre_kT{pcn}"] = ve.v

            blk.vector(pre_ve3)

            for q in range(4):
                def pre_pe3(te, pcn=pcn, q=q):
                    if q == 0:
                        te.wait_ge(ve.h, ck[f"ve_pre_kT{pcn}"])
                        if pcn == 0:
                            te.wait_ge(sc.h, ck["sc_ksig"])
                            te.wait_ge(gpc.h, ck["gp_init"])
                    bq = bank(2 + (q % 2))
                    if q >= 2:
                        te.wait_ge(ve.h, ck[f"ve_pre_stt{pcn}_{q - 2}"])
                    elif pcn >= 1:
                        te.wait_ge(ve.h, ck[f"ve_pre_stt{pcn - 1}_{q + 2}"])
                    vpre = vpreA if pcn % 2 == 0 else vpreB
                    for sbk in range(4):
                        li = te.matmul(bq[:, 0:512], kppTp[:, sbk, :],
                                       vpre[:, sbk, q * 512:(q + 1) * 512],
                                       start=(sbk == 0), stop=(sbk == 3))
                    pe.inc(li)
                    ck[f"pe_pre_st{pcn}_{q}"] = pe.v
                    if q == 3:
                        ck[f"pe_pre_state{pcn}"] = pe.v

                blk.tensor(pre_pe3)

                def pre_ve4(v, pcn=pcn, q=q):
                    v.wait_ge(pe.h, ck[f"pe_pre_st{pcn}_{q}"])
                    ve.inc(v.scalar_tensor_tensor(
                        S_sb[:, q * 512:(q + 1) * 512], S_sb[:, q * 512:(q + 1) * 512],
                        dcol, bank(2 + (q % 2))[:, 0:512], ALU.mult, ALU.add))
                    ck[f"ve_pre_stt{pcn}_{q}"] = ve.v

                blk.vector(pre_ve4)

        ck["ve_prefix_done"] = ve.v

        if _DEBUG_TAPS:
            def tap_pre(gp):
                gp.wait_ge(ve.h, ck["ve_prefix_done"])
                gp.wait_ge(ve.h, ck["ve_prep"])
                d_dbg.dma(gp.dma_start(dbg["S"], S_sb.rearrange("p (a t) -> p a t", a=1)[:, 0, :]))
                d_dbg.dma(gp.dma_start(dbg["ksig"], ksig))
                d_dbg.dma(gp.dma_start(dbg["siluq"], siluq))
                ck["dbg_pre"] = d_dbg.v
            blk.gpsimd(tap_pre)

        emit_gamma(48)

        for c in range(NCH):
            par = c % 2
            pr = c // 2
            cs = slice(c * C, (c + 1) * C)
            cend = E[:, c * C + C - 1:c * C + C]

            if par == 1:
                def gla_gp(gp, c=c):
                    vb = vbuf0 if (c // 2) % 2 == 0 else vbuf1
                    vs = g_vb0 if (c // 2) % 2 == 0 else g_vb1
                    if c >= 5:
                        gp.wait_ge(pe.h, ck[f"pe_oq3_{c - 4}"])
                    gp.wait_ge(g_init.h, ck["gp_init_dma"])
                    vs.dma(gp.dma_start(vb[0:64, :], xtok[64:128, c // 2, :]))
                    ck[f"gp_vb{c}"] = (vs.h, vs.v)

                blk.gpsimd(gla_gp)

            def gla_pe_a(te, c=c, cs=cs):
                if c == 0:
                    te.wait_ge(ve.h, ck["ve_prep"])
                    te.wait_ge(ve.h, ck["ve_prefix_done"])
                    te.wait_ge(s_const.h, ck["sy_consts"])
                    if _DEBUG_TAPS:
                        te.wait_ge(d_dbg.h, ck["dbg_pre"])
                if c >= 1:
                    te.wait_ge(ve.h, ck[f"ve_Pevac{c - 1}"])
                    te.wait_ge(ve.h, ck[f"ve_kTevac{c - 1}"])
                pe.inc(te.matmul(bank(6)[0:64, 0:64], kp[:, cs], qp[:, cs], start=True, stop=True))
                ck[f"pe_P{c}"] = pe.v
                pe.inc(te.transpose(psbf[0:64, 512:640], kpp[:, cs], id_sb[:]))
                ck[f"pe_kT{c}"] = pe.v

            blk.tensor(gla_pe_a)

            def gla_ve_a(v, c=c):
                v.wait_ge(pe.h, ck[f"pe_P{c}"])
                if c >= 1:
                    v.wait_ge(pe.h, ck[f"pe_oq3_{c - 1}"])
                ve.inc(v.tensor_mul(P_sb[0:64, :], bank(6)[0:64, 0:64], mask_sb[:]))
                ck[f"ve_Pevac{c}"] = ve.v
                v.wait_ge(pe.h, ck[f"pe_kT{c}"])
                ve.inc(v.tensor_copy(kppT_sb[0:64, :], psbf[0:64, 512:640]))
                ck[f"ve_kTevac{c}"] = ve.v

            blk.vector(gla_ve_a)

            for q in range(4):
                def gla_pe_b(te, c=c, par=par, pr=pr, cs=cs, q=q):
                    vb = ((vbuf0 if (c // 2) % 2 == 0 else vbuf1)[0:64, :] if par
                          else xtok[0:64, c // 2, :])
                    P_l = P_sb[0:64, :]
                    kT_l = kppT_sb[0:64, :]
                    tp = (0, 64) if par == 1 else None
                    orow = slice(64, 128) if par == 1 else slice(0, 64)
                    if q == 0:
                        te.wait_ge(ve.h, ck[f"ve_kTevac{c}"])
                        te.wait_ge(ve.h, ck[f"ve_Pevac{c}"])
                        if par == 1:
                            vsh, vsv = ck[f"gp_vb{c}"]
                            te.wait_ge(vsh, vsv)
                        if par == 0 and pr >= 1:
                            te.wait_ge(ve.h, ck[f"ve_oevac{pr - 1}"])
                    sbank = bank(4 + (q % 2))
                    if q >= 2:
                        te.wait_ge(ve.h, ck[f"ve_stt{c}_{q - 2}"])
                    elif c >= 1:
                        te.wait_ge(ve.h, ck[f"ve_stt{c - 1}_{q + 2}"])
                    te.matmul(sbank[:, 0:512], kT_l, vb[:, q * 512:(q + 1) * 512],
                              start=True, stop=True)
                    te.matmul(bank(q)[orow, 0:512], P_l, vb[:, q * 512:(q + 1) * 512],
                              start=True, stop=False, tile_position=tp)
                    pe.inc(te.matmul(bank(q)[orow, 0:512], qp[:, cs], S_sb[:, q * 512:(q + 1) * 512],
                                     start=False, stop=True, tile_position=tp))
                    ck[f"pe_oq{q}_{c}"] = pe.v
                    if q == 3:
                        ck[f"pe_oq3_{c}"] = pe.v

                blk.tensor(gla_pe_b)

                def gla_ve_b(v, c=c, cend=cend, q=q):
                    v.wait_ge(pe.h, ck[f"pe_oq{q}_{c}"])
                    ve.inc(v.scalar_tensor_tensor(
                        S_sb[:, q * 512:(q + 1) * 512], S_sb[:, q * 512:(q + 1) * 512],
                        cend, bank(4 + (q % 2))[:, 0:512], ALU.mult, ALU.add))
                    ck[f"ve_stt{c}_{q}"] = ve.v

                blk.vector(gla_ve_b)

            if par == 1:
                def gla_ve_ev(v, c=c, pr=pr):
                    v.wait_ge(pe.h, ck[f"pe_oq3_{c}"])
                    for q in range(4):
                        li = v.tensor_copy(o_tok[:, pr, q * 512:(q + 1) * 512], bank(q)[:, 0:512])
                    ve.inc(li)
                    ck[f"ve_oevac{pr}"] = ve.v

                blk.vector(gla_ve_ev)

        ck["gla_done_ve"] = ve.v

        if _DEBUG_TAPS:
            def tap_o(gp):
                gp.wait_ge(ve.h, ck["gla_done_ve"])
                for tt in range(TT):
                    d_dbg.dma(gp.dma_start(dbg["o"][tt * 128:(tt + 1) * 128, :], o_tok[:, tt, :]))
                ck["dbg_o"] = d_dbg.v
            blk.gpsimd(tap_o)

        def ph5_sc(s):
            for tt in range(TT):
                s.wait_ge(ve.h, ck[f"ve_oevac{tt}"])
                s.wait_ge(sc.h, sc.v)
                sc.inc(s.activation(junk_b[:, 0:2048], o_tok[:, tt, :], AF.Square,
                                    accum_out=oss[:, tt:tt + 1]))
            s.wait_ge(sc.h, sc.v)
            sc.inc(s.activation(lnt[:, 0:8], oss[:, 0:8], AF.Ln, scale=1.0 / H, bias=cst[:, 0:1]))
            s.wait_ge(sc.h, sc.v)
            sc.inc(s.activation(rstdo[:, 0:8], lnt[:, 0:8], AF.Exp, scale=-0.5))
            ck["sc_rstdo"] = sc.v

        blk.scalar(ph5_sc)

        ntr = 0
        for pr in range(TT):
            for vg in range(2):
                def ph5_pe(te, pr=pr, vg=vg, ntr=ntr):
                    te.wait_ge(ve.h, ck[f"ve_oevac{pr}"])
                    if ntr == 0:
                        te.wait_ge(ve.h, ck[f"ve_kTevac{NCH - 1}"])
                    else:
                        te.wait_ge(ve.h, ck[f"ve_oT{ntr - 1}"])
                    for j in range(8):
                        vt = vg * 8 + j
                        li = te.transpose(psbf[:, j * 128:(j + 1) * 128],
                                          o_tok[:, pr, vt * 128:(vt + 1) * 128], id_sb[:])
                    pe.inc(li)
                    ck[f"pe_oT{ntr}"] = pe.v

                blk.tensor(ph5_pe)

                def ph5_ve(v, pr=pr, vg=vg, ntr=ntr):
                    v.wait_ge(pe.h, ck[f"pe_oT{ntr}"])
                    for j in range(8):
                        vt = vg * 8 + j
                        li = v.tensor_copy(oTv[:, vt, pr * 128:(pr + 1) * 128],
                                           psbf[:, j * 128:(j + 1) * 128])
                    ve.inc(li)
                    ck[f"ve_oT{ntr}"] = ve.v

                blk.vector(ph5_ve)
                ntr += 1

        ck["ve_oT_done"] = ve.v

        gseq = 0
        p6_lastuse = {}
        for os_i in range(NOS):
            osl = slice(os_i * OS, (os_i + 1) * OS)
            wb, gb = (wogA, wigA) if os_i % 2 == 0 else (wogB, wigB)

            def ph6_gp(gp, os_i=os_i, wb=wb, gb=gb):
                ws = g_wogA if os_i % 2 == 0 else g_wogB
                if os_i == 1:
                    gp.wait_ge(ve.h, ck["gla_done_ve"])
                if os_i >= 2:
                    gp.wait_ge(pe.h, ck[f"pe_go{os_i - 2}_{TT - 1}"])
                ws.dma(gp.dma_start(
                    wb[:, :, :],
                    d_wog.rearrange("(a p) o -> p a o", p=128)[:, :, os_i * OS:(os_i + 1) * OS]))
                ws.dma(gp.dma_start(
                    gb[:, :, :],
                    d_wig.rearrange("(a p) o -> p a o", p=128)[:, :, os_i * OS:(os_i + 1) * OS]))
                ck[f"gp_wog{os_i}"] = (ws.h, ws.v)

            blk.gpsimd(ph6_gp)

            for tt in range(TT):
                def ph6_pe(te, os_i=os_i, tt=tt, wb=wb, gb=gb):
                    if tt == 0:
                        wsh, wsv = ck[f"gp_wog{os_i}"]
                        te.wait_ge(wsh, wsv)
                        if os_i == 0:
                            te.wait_ge(ve.h, ck["ve_oT_done"])
                    prev = p6_lastuse.get(tt % 6)
                    if prev is not None:
                        te.wait_ge(sc.h, ck[f"sc_sg{prev[0]}_{prev[1]}"])
                    p6_lastuse[tt % 6] = (os_i, tt)
                    bk = bank(tt % 6)
                    for vt in range(HT):
                        te.matmul(bk[:, 0:OS], oTv[:, vt, tt * 128:(tt + 1) * 128],
                                  wb[:, vt, :], start=(vt == 0), stop=False)
                        li = te.matmul(bk[:, 0:OS], xT[:, vt, tt * 128:(tt + 1) * 128],
                                       gb[:, vt, :], start=False, stop=(vt == HT - 1))
                    pe.inc(li)
                    ck[f"pe_go{os_i}_{tt}"] = pe.v

                blk.tensor(ph6_pe)

                def ph6_sc(s, os_i=os_i, tt=tt):
                    s.wait_ge(pe.h, ck[f"pe_go{os_i}_{tt}"])
                    cur = os_i * TT + tt
                    if cur >= 2:
                        po, pt = divmod(cur - 2, TT)
                        s.wait_ge(ve.h, ck[f"ve_ep{po}_{pt}"])
                    sgb = sg0 if tt % 2 == 0 else sg1
                    sc.inc(s.activation(sgb, bank(tt % 6)[:, 0:OS], AF.Silu))
                    ck[f"sc_sg{os_i}_{tt}"] = sc.v

                blk.scalar(ph6_sc)
                emit_gamma(3 if gseq % 2 == 0 else 4)
                gseq += 1

                def ph6_ve(v, os_i=os_i, tt=tt, osl=osl):
                    v.wait_ge(sc.h, ck[f"sc_sg{os_i}_{tt}"])
                    if _DEBUG_TAPS and os_i == 0 and tt == 0:
                        v.wait_ge(d_dbg.h, ck["dbg_o"])
                    sgb = sg0 if tt % 2 == 0 else sg1
                    v.wait_ge(ve.h, ve.v)
                    ve.inc(v.scalar_tensor_tensor(og6, o_tok[:, tt, osl], rstdo[:, tt:tt + 1],
                                                  gw_b[:, osl], ALU.mult, ALU.mult))
                    v.wait_ge(ve.h, ve.v)
                    ve.inc(v.scalar_tensor_tensor(
                        o_tok[:, tt, osl], sgb, 1.0, og6, ALU.mult, ALU.mult))
                    ck[f"ve_ep{os_i}_{tt}"] = ve.v

                blk.vector(ph6_ve)

        emit_gamma(NGALL)

        for tt in range(TT):
            def ph6b_gp(gp, tt=tt):
                xr = xres0 if tt % 2 == 0 else xres1
                xsem = g_xr0 if tt % 2 == 0 else g_xr1
                if tt < 2:
                    gp.wait_ge(ve.h, ck["ve_prep"])
                else:
                    gp.wait_ge(ve.h, ck[f"ve_z{tt - 2}"])
                xsem.dma(gp.dma_start(xr, d_x[tt * 128:(tt + 1) * 128, :]))
                ck[f"gp_xres{tt}"] = (xsem.h, xsem.v)

            blk.gpsimd(ph6b_gp)

            def ph6b_ve(v, tt=tt):
                xr = xres0 if tt % 2 == 0 else xres1
                xsh, xsv = ck[f"gp_xres{tt}"]
                v.wait_ge(xsh, xsv)
                v.wait_ge(ve.h, ck[f"ve_ep{NOS - 1}_{tt}"])
                ve.inc(v.scalar_tensor_tensor(
                    o_tok[:, tt, :], xr, 1.0, o_tok[:, tt, :], ALU.mult, ALU.add,
                    accum_out=sum_c[:, tt:tt + 1]))
                ck[f"ve_z{tt}"] = ve.v

            blk.vector(ph6b_ve)

            def ph6b_sc(s, tt=tt):
                s.wait_ge(ve.h, ck[f"ve_z{tt}"])
                s.wait_ge(sc.h, sc.v)
                sc.inc(s.activation(junk_b[:, 0:2048], o_tok[:, tt, :], AF.Square,
                                    accum_out=sq_c[:, tt:tt + 1]))
                ck[f"sc_zsq{tt}"] = sc.v

            blk.scalar(ph6b_sc)

        def ph6b_sc2a(s):
            s.wait_ge(sc.h, ck[f"sc_zsq{TT - 1}"])
            sc.inc(s.activation(mean_c[:, 0:TT], sum_c[:, 0:TT], AF.Copy, scale=1.0 / H))
            sc.inc(s.activation(e2_c[:, 0:TT], sq_c[:, 0:TT], AF.Copy, scale=1.0 / H))
            s.wait_ge(sc.h, sc.v)
            sc.inc(s.activation(m2_c[:, 0:TT], mean_c[:, 0:TT], AF.Square))
            s.wait_ge(sc.h, sc.v)
            sc.inc(s.activation(nm2_c[:, 0:TT], m2_c[:, 0:TT], AF.Copy, scale=-1.0, bias=EPS))
            ck["sc_ln1a"] = sc.v

        blk.scalar(ph6b_sc2a)

        def ph6b_ping(v):
            v.wait_ge(sc.h, ck["sc_ln1a"])
            ve.inc(v.nop())
            ck["ve_ln1ping"] = ve.v

        blk.vector(ph6b_ping)

        def ph6b_sc2b(s):
            s.wait_ge(ve.h, ck["ve_ln1ping"])
            for tt in range(TT):
                sc.inc(s.activation(lnt[:, tt:tt + 1], e2_c[:, tt:tt + 1], AF.Ln,
                                    bias=nm2_c[:, tt:tt + 1]))
            s.wait_ge(sc.h, sc.v)
            sc.inc(s.activation(r_c[:, 0:TT], lnt[:, 0:TT], AF.Exp, scale=-0.5))
            ck["sc_ln1"] = sc.v

        blk.scalar(ph6b_sc2b)

        def ph6b_ve2(v):
            v.wait_ge(sc.h, ck["sc_ln1"])
            for tt in range(TT):
                li = v.tensor_scalar(o_tok[:, tt, :], o_tok[:, tt, :],
                                     mean_c[:, tt:tt + 1], r_c[:, tt:tt + 1],
                                     ALU.subtract, ALU.mult)
            ve.inc(li)
            ck["ve_y1"] = ve.v

        blk.vector(ph6b_ve2)

        if _DEBUG_TAPS:
            def tap_y1(gp):
                gp.wait_ge(ve.h, ck["ve_y1"])
                for tt in range(TT):
                    d_dbg.dma(gp.dma_start(dbg["y1"][tt * 128:(tt + 1) * 128, :], o_tok[:, tt, :]))
                ck["dbg_y1"] = d_dbg.v
            blk.gpsimd(tap_y1)

        def gfin_pe(te):
            te.wait_ge(ve.h, ck[f"g_red{NGALL - 1}"])
            te.wait_ge(sc.h, ck[f"sc_sg{NOS - 1}_{TT - 1}"])
            pe.inc(te.matmul(bank(4)[0:1, 0:NGALL], ones_cf[:], gpart[:, 0:NGALL], start=True, stop=True))
            ck["pe_gsum"] = pe.v

        blk.tensor(gfin_pe)

        def gfin_ve(v):
            v.wait_ge(pe.h, ck["pe_gsum"])
            for m in range(3):
                a0, a1 = gcols[m]
                li = v.tensor_reduce(grow[0:1, m:m + 1], bank(4)[0:1, a0:a1], AX.X, ALU.add)
            ve.inc(li)
            ck["ve_gsum"] = ve.v

        blk.vector(gfin_ve)

        def gfin_sc(s):
            s.wait_ge(ve.h, ck["ve_gsum"])
            sc.inc(s.activation(throw[0:1, 0:3], grow[0:1, 0:3], AF.Copy,
                                scale=0.5 / QCNT, bias=0.5e-8))
            sc.inc(s.activation(nthrow[0:1, 0:3], grow[0:1, 0:3], AF.Copy,
                                scale=-0.5 / QCNT, bias=-0.5e-8))
            ck["sc_th"] = sc.v

        blk.scalar(gfin_sc)

        def gfin_pe2(te):
            te.wait_ge(sc.h, ck["sc_th"])
            te.matmul(bank(4)[:, 0:3], ones_rf[:], throw[0:1, 0:3], start=True, stop=True)
            pe.inc(te.matmul(bank(4)[:, 16:19], ones_rf[:], nthrow[0:1, 0:3], start=True, stop=True))
            ck["pe_thb"] = pe.v

        blk.tensor(gfin_pe2)

        def gfin_sc2(s):
            s.wait_ge(pe.h, ck["pe_thb"])
            sc.inc(s.copy(th_c[:, 0:3], bank(4)[:, 0:3]))
            sc.inc(s.copy(nth_c[:, 0:3], bank(4)[:, 16:19]))
            ck["sc_thc"] = sc.v

        blk.scalar(gfin_sc2)

        def gfin_ve2(v):
            v.wait_ge(sc.h, ck["sc_thc"])
            ve.inc(v.tensor_copy(og6[:, 0:1], ones_cb[:]))
            ck["ve_th"] = ve.v

        blk.vector(gfin_ve2)

        if _DEBUG_TAPS:
            blk.gpsimd(lambda gp: (
                gp.wait_ge(ve.h, ck["ve_th"]),
                d_dbg.dma(gp.dma_start(dbg["th"][:, 0:3], th_c[:, 0:3])),
                d_dbg.dma(gp.dma_start(dbg["th"][:, 4:7], nth_c[:, 0:3])),
                ck.__setitem__("dbg_th", d_dbg.v)))

        ntr = 0
        for pr in range(TT):
            for vg in range(2):
                def ph7_pe(te, pr=pr, vg=vg, ntr=ntr):
                    if ntr == 0:
                        te.wait_ge(ve.h, ck["ve_y1"])
                        te.wait_ge(ve.h, ck["ve_th"])
                        te.wait_ge(sc.h, ck["sc_ln1"])
                    else:
                        te.wait_ge(ve.h, ck[f"ve_y1T{ntr - 1}"])
                    for j in range(8):
                        vt = vg * 8 + j
                        li = te.transpose(psbf[:, j * 128:(j + 1) * 128],
                                          o_tok[:, pr, vt * 128:(vt + 1) * 128], id_sb[:])
                    pe.inc(li)
                    ck[f"pe_y1T{ntr}"] = pe.v

                blk.tensor(ph7_pe)

                def ph7_ve(v, pr=pr, vg=vg, ntr=ntr):
                    v.wait_ge(pe.h, ck[f"pe_y1T{ntr}"])
                    for j in range(8):
                        vt = vg * 8 + j
                        li = v.tensor_copy(y1T[:, vt, pr * 128:(pr + 1) * 128],
                                           psbf[:, j * 128:(j + 1) * 128])
                    ve.inc(li)
                    ck[f"ve_y1T{ntr}"] = ve.v

                blk.vector(ph7_ve)
                ntr += 1

        ck["ve_y1T_done"] = ve.v

        for ft in range(FT):
            wr1, wr2 = (w1rA, w2rA) if ft % 2 == 0 else (w1rB, w2rB)

            def ph8_sy(sy, ft=ft, wr1=wr1, wr2=wr2):
                wsem = s_w12A if ft % 2 == 0 else s_w12B
                if ft < 2:
                    sy.wait_ge(ve.h, ck["ve_y1T_done"])
                else:
                    sy.wait_ge(ve.h, ck[f"ve_q12_{ft - 2}"])
                wsem.dma(sy.dma_start(
                    wr1[:, :, :],
                    d_w1.rearrange("(a p) f -> p a f", p=128)[:, :, ft * 128:(ft + 1) * 128]))
                wsem.dma(sy.dma_start(
                    wr2[:, :, :],
                    d_w2.rearrange("(a p) f -> p a f", p=128)[:, :, ft * 128:(ft + 1) * 128]))
                ck[f"sy_w12_{ft}"] = (wsem.h, wsem.v)

            blk.sync(ph8_sy)

            def ph8_ve_q(v, ft=ft, wr1=wr1, wr2=wr2):
                wq1_sb, wq2_sb = (wq1A, wq2A) if ft % 2 == 0 else (wq1B, wq2B)
                wsh, wsv = ck[f"sy_w12_{ft}"]
                v.wait_ge(wsh, wsv)
                if ft == 0:
                    v.wait_ge(ve.h, ck["ve_th"])
                    v.wait_ge(ve.h, ck["ve_y1T_done"])
                    if _DEBUG_TAPS:
                        v.wait_ge(d_dbg.h, ck["dbg_th"])
                if ft >= 2:
                    v.wait_ge(pe.h, ck[f"pe_x12_{ft - 2}"])
                for wraw, wqd, m in ((wr1, wq1_sb, 0), (wr2, wq2_sb, 1)):
                    src = wraw.rearrange("p a t -> p (a t)")
                    dst = wqd.rearrange("p a t -> p (a t)")
                    v.wait_ge(ve.h, ve.v)
                    ve.inc(v.tensor_single_scalar(dst, src, nth_c[:, m:m + 1], ALU.is_le))
                    ve.inc(v.tensor_single_scalar(pq8, src, th_c[:, m:m + 1], ALU.is_ge))
                    v.wait_ge(ve.h, ve.v)
                    li = v.tensor_sub(dst, pq8, dst)
                    ve.inc(li)
                ck[f"ve_q12_{ft}"] = ve.v

            blk.vector(ph8_ve_q)

            def ph8_pe(te, ft=ft):
                wq1_sb, wq2_sb = (wq1A, wq2A) if ft % 2 == 0 else (wq1B, wq2B)
                te.wait_ge(ve.h, ck[f"ve_q12_{ft}"])
                bks = [(4 * ft + j) % 7 for j in range(4)]
                for j in range(4):
                    g = 4 * ft + j - 7
                    if g >= 0:
                        pft, pj = divmod(g, 4)
                        if pj < 2:
                            te.wait_ge(sc.h, ck[f"sc_sgf{pft}"])
                        else:
                            te.wait_ge(ve.h, ck[f"ve_h{pft}_{pj - 2}"])
                for half in range(2):
                    for hh in range(HT):
                        te.matmul(bank(bks[half])[:, 0:512], wq1_sb[:, hh, :],
                                  y1T[:, hh, half * 512:(half + 1) * 512],
                                  start=(hh == 0), stop=(hh == HT - 1))
                for half in range(2):
                    for hh in range(HT):
                        li = te.matmul(bank(bks[2 + half])[:, 0:512], wq2_sb[:, hh, :],
                                       y1T[:, hh, half * 512:(half + 1) * 512],
                                       start=(hh == 0), stop=(hh == HT - 1))
                pe.inc(li)
                ck[f"pe_x12_{ft}"] = pe.v

            blk.tensor(ph8_pe)

            def ph8_sc(s, ft=ft):
                s.wait_ge(pe.h, ck[f"pe_x12_{ft}"])
                if ft >= 1:
                    s.wait_ge(ve.h, ck[f"ve_h{ft - 1}_1"])
                bks = [(4 * ft + j) % 7 for j in range(4)]
                sc.inc(s.activation(sgf0, bank(bks[0])[:, 0:512], AF.Sigmoid))
                sc.inc(s.activation(sgf1, bank(bks[1])[:, 0:512], AF.Sigmoid))
                ck[f"sc_sgf{ft}"] = sc.v

            blk.scalar(ph8_sc)

            def ph8_ve_h(v, ft=ft):
                v.wait_ge(sc.h, ck[f"sc_sgf{ft}"])
                bks = [(4 * ft + j) % 7 for j in range(4)]
                ve.inc(v.tensor_mul(h_sb[:, ft, 0:512], sgf0, bank(bks[2])[:, 0:512]))
                ck[f"ve_h{ft}_0"] = ve.v
                ve.inc(v.tensor_mul(h_sb[:, ft, 512:1024], sgf1, bank(bks[3])[:, 0:512]))
                ck[f"ve_h{ft}_1"] = ve.v

            blk.vector(ph8_ve_h)

        if _DEBUG_TAPS:
            blk.gpsimd(lambda gp: (
                gp.wait_ge(ve.h, ck["ve_h0_1"]),
                d_dbg.dma(gp.dma_start(dbg["h0"], h_sb[:, 0, :]))))

        for dt in range(DT):
            wq3 = w3qA if dt % 2 == 0 else w3qB

            def ph9_sy(sy, dt=dt):
                if dt == 0:
                    sy.wait_ge(ve.h, ck[f"ve_q12_{FT - 1}"])
                else:
                    sy.wait_ge(ve.h, ck[f"ve_q3_{dt - 1}"])
                s_w3.dma(sy.dma_start(
                    w3r[:, :, :],
                    d_w3.rearrange("(a p) d -> p a d", p=128)[:, :, dt * 128:(dt + 1) * 128]))
                ck[f"sy_w3_{dt}"] = s_w3.v

            blk.sync(ph9_sy)

            def ph9_ve_q(v, dt=dt, wq3=wq3):
                v.wait_ge(s_w3.h, ck[f"sy_w3_{dt}"])
                if dt == 0:
                    v.wait_ge(ve.h, ck[f"ve_h{FT - 1}_1"])
                if dt == 1:
                    v.wait_ge(pe.h, ck[f"pe_x12_{FT - 1}"])
                if dt >= 2:
                    v.wait_ge(pe.h, ck[f"pe_f3_{dt - 2}"])
                src = w3r.rearrange("p a t -> p (a t)")
                dst = wq3.rearrange("p a t -> p (a t)")
                for a, b_ in ((0, 2048), (2048, 4096), (4096, 5632)):
                    w = b_ - a
                    v.wait_ge(ve.h, ve.v)
                    ve.inc(v.tensor_single_scalar(dst[:, a:b_], src[:, a:b_], nth_c[:, 2:3], ALU.is_le))
                    ve.inc(v.tensor_single_scalar(pq9[:, 0:w], src[:, a:b_], th_c[:, 2:3], ALU.is_ge))
                    v.wait_ge(ve.h, ve.v)
                    ve.inc(v.tensor_sub(dst[:, a:b_], pq9[:, 0:w], dst[:, a:b_]))
                ck[f"ve_q3_{dt}"] = ve.v

            blk.vector(ph9_ve_q)

            def ph9_pe(te, dt=dt, wq3=wq3):
                te.wait_ge(ve.h, ck[f"ve_q3_{dt}"])
                if dt >= 2:
                    te.wait_ge(ve.h, ck[f"ve_z2_{dt - 2}"])
                b0 = bank(0 if dt % 2 == 0 else 2)
                b1 = bank(1 if dt % 2 == 0 else 3)
                for ftl in range(FT):
                    te.matmul(b0[:, 0:512], wq3[:, ftl, :], h_sb[:, ftl, 0:512],
                              start=(ftl == 0), stop=(ftl == FT - 1))
                    li = te.matmul(b1[:, 0:512], wq3[:, ftl, :], h_sb[:, ftl, 512:1024],
                                   start=(ftl == 0), stop=(ftl == FT - 1))
                pe.inc(li)
                ck[f"pe_f3_{dt}"] = pe.v

            blk.tensor(ph9_pe)

            def ph9_ve_z2(v, dt=dt):
                v.wait_ge(pe.h, ck[f"pe_f3_{dt}"])
                b0 = bank(0 if dt % 2 == 0 else 2)
                b1 = bank(1 if dt % 2 == 0 else 3)
                v.scalar_tensor_tensor(z2[:, dt, 0:512], y1T[:, dt, 0:512], 1.0,
                                       b0[:, 0:512], ALU.mult, ALU.add)
                ve.inc(v.scalar_tensor_tensor(z2[:, dt, 512:1024], y1T[:, dt, 512:1024], 1.0,
                                              b1[:, 0:512], ALU.mult, ALU.add))
                ck[f"ve_z2_{dt}"] = ve.v

            blk.vector(ph9_ve_z2)

            def ph9_sc(s, dt=dt):
                s.wait_ge(ve.h, ck[f"ve_z2_{dt}"])
                if dt >= 1:
                    s.wait_ge(pe.h, ck[f"pe_l2s_{dt - 1}"])
                s.wait_ge(sc.h, sc.v)
                sc.inc(s.activation(junk9[:, 0:1024], z2[:, dt, :], AF.Square))
                ck[f"sc_z2sq{dt}"] = sc.v

            blk.scalar(ph9_sc)

            def ph9_pe_st(te, dt=dt):
                te.wait_ge(sc.h, ck[f"sc_z2sq{dt}"])
                te.matmul(ps[0:1, 2048:2560], ones_cb[:], z2[:, dt, 0:512],
                          start=(dt == 0), stop=(dt == DT - 1))
                te.matmul(ps[0:1, 2560:3072], ones_cb[:], z2[:, dt, 512:1024],
                          start=(dt == 0), stop=(dt == DT - 1))
                te.matmul(ps[0:1, 3072:3584], ones_cb[:], junk9[:, 0:512],
                          start=(dt == 0), stop=(dt == DT - 1))
                pe.inc(te.matmul(ps[0:1, 3584:4096], ones_cb[:], junk9[:, 512:1024],
                                 start=(dt == 0), stop=(dt == DT - 1)))
                ck[f"pe_l2s_{dt}"] = pe.v

            blk.tensor(ph9_pe_st)

        def ph9b_ve(v):
            v.wait_ge(pe.h, ck[f"pe_l2s_{DT - 1}"])
            v.tensor_copy(sums_r[0:1, :], ps[0:1, 2048:3072])
            ve.inc(v.tensor_copy(ssq_r[0:1, :], ps[0:1, 3072:4096]))
            ck["ve_l2rows"] = ve.v

        blk.vector(ph9b_ve)

        def ph9b_sc(s):
            s.wait_ge(ve.h, ck["ve_l2rows"])
            sc.inc(s.activation(ln2a[0:1, :], sums_r[0:1, :], AF.Copy, scale=1.0 / H))
            sc.inc(s.activation(ln2b[0:1, :], ssq_r[0:1, :], AF.Copy, scale=1.0 / H))
            s.wait_ge(sc.h, sc.v)
            sc.inc(s.activation(sums_r[0:1, :], ln2a[0:1, :], AF.Square))
            ck["sc_l2a"] = sc.v

        blk.scalar(ph9b_sc)

        def ph9b_ve2(v):
            v.wait_ge(sc.h, ck["sc_l2a"])
            ve.inc(v.tensor_sub(ssq_r[0:1, :], ln2b[0:1, :], sums_r[0:1, :]))
            ck["ve_l2b"] = ve.v

        blk.vector(ph9b_ve2)

        def ph9b_sc2(s):
            s.wait_ge(ve.h, ck["ve_l2b"])
            sc.inc(s.activation(ln2b[0:1, :], ssq_r[0:1, :], AF.Ln, bias=cst[0:1, 0:1]))
            s.wait_ge(sc.h, sc.v)
            sc.inc(s.activation(ssq_r[0:1, :], ln2b[0:1, :], AF.Exp, scale=-0.5))
            ck["sc_l2r"] = sc.v

        blk.scalar(ph9b_sc2)

        def ph9b_pe(te):
            te.wait_ge(sc.h, ck["sc_l2r"])
            te.matmul(bank(4)[:, 0:512], ones_rf[:], ln2a[0:1, 0:512], start=True, stop=True)
            te.matmul(bank(5)[:, 0:512], ones_rf[:], ln2a[0:1, 512:1024], start=True, stop=True)
            te.matmul(bank(6)[:, 0:512], ones_rf[:], ssq_r[0:1, 0:512], start=True, stop=True)
            pe.inc(te.matmul(bank(0)[:, 0:512], ones_rf[:], ssq_r[0:1, 512:1024], start=True, stop=True))
            ck["pe_l2b"] = pe.v

        blk.tensor(ph9b_pe)

        def ph9b_ve3(v):
            v.wait_ge(pe.h, ck["pe_l2b"])
            v.tensor_copy(mb_sb[:, 0:512], bank(4)[:, 0:512])
            v.tensor_copy(mb_sb[:, 512:1024], bank(5)[:, 0:512])
            v.tensor_copy(rb_sb[:, 0:512], bank(6)[:, 0:512])
            ve.inc(v.tensor_copy(rb_sb[:, 512:1024], bank(0)[:, 0:512]))
            ck["ve_l2bc"] = ve.v

        blk.vector(ph9b_ve3)

        if _DEBUG_TAPS:
            blk.gpsimd(lambda gp: (
                gp.wait_ge(ve.h, ck["ve_l2bc"]),
                d_dbg.dma(gp.dma_start(dbg["z2"], z2[:, 0, :]))))

        for dt in range(DT):
            def ph9c_ve(v, dt=dt):
                if dt == 0:
                    v.wait_ge(ve.h, ck["ve_l2bc"])
                else:
                    v.wait_ge(o_dma.h, ck[f"od_{dt - 1}"])
                ve.inc(v.scalar_tensor_tensor(y2t, z2[:, dt, :], 1.0, mb_sb, ALU.mult, ALU.subtract))
                v.wait_ge(ve.h, ve.v)
                ve.inc(v.tensor_mul(y2t, y2t, rb_sb))
                ck[f"ve_y2_{dt}"] = ve.v

            blk.vector(ph9c_ve)

            def ph9c_sy(sy, dt=dt):
                sy.wait_ge(ve.h, ck[f"ve_y2_{dt}"])
                o_dma.dma(sy.dma_start(d_y[dt * 128:(dt + 1) * 128, :], y2t))
                ck[f"od_{dt}"] = o_dma.v

            blk.sync(ph9c_sy)

        def fin_sync(sy):
            sy.wait_ge(o_dma.h, o_dma.v)

        blk.sync(fin_sync)

        def fin_gp(gp):
            gp.wait_ge(o_dma.h, o_dma.v)
            if _DEBUG_TAPS:
                gp.wait_ge(d_dbg.h, d_dbg.v)

        blk.gpsimd(fin_gp)

    return nc


_CACHED = None


def kernel(L0_attn_hidden_states, L0_attn_Wq, L0_attn_Wk, L0_attn_Wog,
           L0_attn_Wig, L0_attn_g_weight, L0_ffn_w1, L0_ffn_w2, L0_ffn_w3):
    global _CACHED
    x = np.ascontiguousarray(L0_attn_hidden_states, dtype=np.float32)
    wqT = np.ascontiguousarray(np.asarray(L0_attn_Wq, dtype=np.float32).T)
    wkT = np.ascontiguousarray(np.asarray(L0_attn_Wk, dtype=np.float32).T)
    wogT = np.ascontiguousarray(np.asarray(L0_attn_Wog, dtype=np.float32).T)
    wigT = np.ascontiguousarray(np.asarray(L0_attn_Wig, dtype=np.float32).T)
    w1T = np.ascontiguousarray(np.asarray(L0_ffn_w1, dtype=np.float32).T)
    w2T = np.ascontiguousarray(np.asarray(L0_ffn_w2, dtype=np.float32).T)
    w3T = np.ascontiguousarray(np.asarray(L0_ffn_w3, dtype=np.float32).T)
    gw = np.ascontiguousarray(np.asarray(L0_attn_g_weight, dtype=np.float32))
    identb = np.eye(128, dtype=bfdt)
    masksb = np.triu(np.ones((64, 64), dtype=np.float32))

    in_maps = []
    for core in range(8):
        b, s = divmod(core, 4)
        t0 = s * T
        xs = np.ascontiguousarray(x[b, t0:t0 + T, :])
        xsT = np.ascontiguousarray(xs.T)
        xp = np.zeros((PRE, H), dtype=np.float32)
        if t0 > 0:
            xp[PRE - t0:] = x[b, 0:t0, :]
        xpT = np.ascontiguousarray(xp.T)
        in_maps.append({
            "x_seg": xs, "x_segT": xsT, "x_pre": xp, "x_preT": xpT,
            "wqT": wqT, "wkT": wkT, "wogT": wogT, "wigT": wigT,
            "w1T": w1T, "w2T": w2T, "w3T": w3T, "g_w": gw,
            "identb": identb, "masksb": masksb,
        })

    if _CACHED is None:
        _CACHED = _build()
    res = run_bass_kernel_spmd(_CACHED, in_maps, core_ids=list(range(8)))
    out = np.empty((B, L, H), dtype=np.float32)
    for core in range(8):
        b, s = divmod(core, 4)
        out[b, s * T:(s + 1) * T, :] = np.asarray(res.results[core]["y_out"]).T
    return out


# version 04 (final)
# speedup vs baseline: 1.2465x; 1.1749x over previous
import numpy as np
import ml_dtypes
from contextlib import ExitStack

import concourse.bass as bass
import concourse.mybir as mybir
from concourse.bass_utils import run_bass_kernel_spmd

F32 = mybir.dt.float32
BF16 = mybir.dt.bfloat16
AF = mybir.ActivationFunctionType
ALU = mybir.AluOpType
AX = mybir.AxisListType
bfdt = ml_dtypes.bfloat16

B, L, H, KD, FF = 2, 4096, 2048, 128, 5632
T = 1024
PRE = 3072
PC = 512
NPC = PRE // PC
C = 64
NCH = T // C
HT = H // 128
TT = T // 128
FT = FF // 128
DT = H // 128
OS = 256
NOS = H // OS
SCALE = float(KD) ** -0.5
EPS = 1e-5
QCNT = float(FF * H)
GT = 1024

_DEBUG_TAPS = False


class Sem:
    def __init__(self, ctx, nc, name):
        self.h = ctx.enter_context(nc.semaphore(name))
        self.v = 0

    def inc(self, inst, n=1):
        inst.then_inc(self.h, n)
        self.v += n
        return self.v

    def dma(self, inst):
        return self.inc(inst, 16)


def _build():
    nc = bass.Bass("TRN2", target_bir_lowering=False)

    d_x = nc.dram_tensor("x_seg", [T, H], F32, kind="ExternalInput").ap()
    d_xT = nc.dram_tensor("x_segT", [H, T], F32, kind="ExternalInput").ap()
    d_xp = nc.dram_tensor("x_pre", [PRE, H], F32, kind="ExternalInput").ap()
    d_xpT = nc.dram_tensor("x_preT", [H, PRE], F32, kind="ExternalInput").ap()
    d_wq = nc.dram_tensor("wqT", [H, KD], F32, kind="ExternalInput").ap()
    d_wk = nc.dram_tensor("wkT", [H, KD], F32, kind="ExternalInput").ap()
    d_wog = nc.dram_tensor("wogB", [NOS, 128, HT * OS], F32, kind="ExternalInput").ap()
    d_wig = nc.dram_tensor("wigB", [NOS, 128, HT * OS], F32, kind="ExternalInput").ap()
    d_w1 = nc.dram_tensor("w1B", [FT, 128, HT * 128], F32, kind="ExternalInput").ap()
    d_w2 = nc.dram_tensor("w2B", [FT, 128, HT * 128], F32, kind="ExternalInput").ap()
    d_w3 = nc.dram_tensor("w3B", [DT, 128, FT * 128], F32, kind="ExternalInput").ap()
    d_gw = nc.dram_tensor("g_w", [H], F32, kind="ExternalInput").ap()
    d_id = nc.dram_tensor("identb", [128, 128], BF16, kind="ExternalInput").ap()
    d_mask = nc.dram_tensor("masksb", [64, 64], F32, kind="ExternalInput").ap()
    d_y = nc.dram_tensor("y_out", [H, T], F32, kind="ExternalOutput").ap()

    dbg = {}
    if _DEBUG_TAPS:
        dbg["S"] = nc.dram_tensor("dbg_S", [KD, H], F32, kind="ExternalOutput").ap()
        dbg["o"] = nc.dram_tensor("dbg_o", [T, H], F32, kind="ExternalOutput").ap()
        dbg["y1"] = nc.dram_tensor("dbg_y1", [T, H], F32, kind="ExternalOutput").ap()
        dbg["th"] = nc.dram_tensor("dbg_th", [128, 8], F32, kind="ExternalOutput").ap()
        dbg["ksig"] = nc.dram_tensor("dbg_ksig", [KD, T], F32, kind="ExternalOutput").ap()
        dbg["siluq"] = nc.dram_tensor("dbg_siluq", [KD, T], F32, kind="ExternalOutput").ap()
        dbg["h0"] = nc.dram_tensor("dbg_h0", [128, T], F32, kind="ExternalOutput").ap()
        dbg["z2"] = nc.dram_tensor("dbg_z2", [128, T], F32, kind="ExternalOutput").ap()

    ctx = ExitStack()
    with ctx:
        def sbt(name, shape, dt):
            return ctx.enter_context(nc.sbuf_tensor(name, shape, dt))

        wq_sb = sbt("wq_sb", [128, HT, 128], BF16)
        wk_sb = sbt("wk_sb", [128, HT, 128], BF16)
        gw_b = sbt("gw_b", [128, H], BF16)
        id_sb = sbt("id_sb", [128, 128], BF16)
        mask_sb = sbt("mask_sb", [64, 64], F32)
        ones_cb = sbt("ones_cb", [128, 1], BF16)
        ones_cf = sbt("ones_cf", [128, 1], F32)
        ones_rf = sbt("ones_rf", [1, 128], F32)
        cst = sbt("cst", [128, 3], F32)
        nc.const_aps.register(F32, EPS, cst[:, 0:1])
        nc.const_aps.register(F32, 0.5e-8, cst[:, 1:2])
        nc.const_aps.register(F32, -0.5e-8, cst[:, 2:3])

        rbig = sbt("rbig", [128, 45056], BF16)
        xT = rbig[:, 0:16384].rearrange("p (a t) -> p a t", a=HT)
        xtok = rbig[:, 16384:32768].rearrange("p (a t) -> p a t", a=TT)
        oTv = rbig[:, 16384:32768].rearrange("p (a t) -> p a t", a=HT)
        qp = rbig[:, 32768:33792]
        kp = rbig[:, 33792:34816]
        kpp = rbig[:, 34816:35840]
        S_sb = rbig[:, 35840:37888]
        vbuf0 = rbig[:, 37888:39936]
        vbuf1 = rbig[:, 39936:41984]
        kppT_sb = rbig[:, 41984:42112]
        P_sb = rbig[:, 42112:42176]
        kppTp = rbig[:, 42176:42688].rearrange("p (a t) -> p a t", a=4)
        kppre = rbig[:, 42688:43200]
        wogB = rbig[:, 32768:36864].rearrange("p (a t) -> p a t", a=HT)
        wigB = rbig[:, 36864:40960].rearrange("p (a t) -> p a t", a=HT)
        h_sb = rbig[:, 0:45056].rearrange("p (a t) -> p a t", a=FT)

        rb2 = sbt("rb2", [128, 16384], BF16)
        o_tok = rb2[:, 0:16384].rearrange("p (a t) -> p a t", a=TT)
        vpreA = rb2[:, 0:8192].rearrange("p (a t) -> p a t", a=4)
        xpreTA = rb2[:, 8192:16384].rearrange("p (a t) -> p a t", a=HT)
        wq1A = rb2[:, 0:2048].rearrange("p (a t) -> p a t", a=HT)
        wq2A = rb2[:, 2048:4096].rearrange("p (a t) -> p a t", a=HT)
        wq1B = rb2[:, 11264:13312].rearrange("p (a t) -> p a t", a=HT)
        wq2B = rb2[:, 13312:15360].rearrange("p (a t) -> p a t", a=HT)
        pq8 = rb2[:, 4096:6144]
        w3qA = rb2[:, 4096:9728].rearrange("p (a t) -> p a t", a=FT)
        w3qB = rb2[:, 9728:15360].rearrange("p (a t) -> p a t", a=FT)
        pq9 = rb2[:, 0:2048]
        junk9 = rb2[:, 15360:16384]

        rh = sbt("rh", [128, 16384], BF16)
        y1T = rh[:, 0:16384].rearrange("p (a t) -> p a t", a=HT)
        z2 = y1T
        junk_b = rh[:, 0:2048]
        vpreB = rh[:, 0:8192].rearrange("p (a t) -> p a t", a=4)
        xpreTB = rh[:, 8192:16384].rearrange("p (a t) -> p a t", a=HT)
        og6 = rh[:, 2048:2304]
        wogA = rh[:, 4096:8192].rearrange("p (a t) -> p a t", a=HT)
        wigA = rh[:, 8192:12288].rearrange("p (a t) -> p a t", a=HT)

        rf = sbt("rf", [128, 10752], F32)
        siluq = rf[:, 0:1024]
        ksig = rf[:, 1024:2048]
        sp = rf[:, 2048:3072]
        ccum = rf[:, 3072:4096]
        E = rf[:, 4096:5120]
        Einv = rf[:, 5120:6144]
        kspre = rf[:, 6144:6656]
        sppre = rf[:, 6656:7168]
        ccpre = rf[:, 7168:7680]
        kfac = rf[:, 7680:8192]
        gam0 = rf[:, 6144:8192]
        gam1 = rf[:, 8192:10240]
        stats = rf[:, 10240:10752]
        gpart = stats[:, 0:288]
        th_c = stats[:, 288:292]
        nth_c = stats[:, 292:296]
        oss = stats[:, 296:304]
        rstdo = stats[:, 304:312]
        lnt = stats[:, 312:320]
        sum_c = stats[:, 320:328]
        sq_c = stats[:, 328:336]
        mean_c = stats[:, 336:344]
        e2_c = stats[:, 344:352]
        m2_c = stats[:, 352:360]
        nm2_c = stats[:, 360:368]
        r_c = stats[:, 368:376]
        dcol = stats[:, 376:377]
        ncol = stats[:, 377:378]
        grow = stats[:, 384:388]
        throw = stats[:, 388:392]
        nthrow = stats[:, 392:396]
        xres0 = rf[:, 0:2048]
        xres1 = rf[:, 2048:4096]
        sg0 = rf[:, 4096:4352]
        sg1 = rf[:, 4352:4608]
        w1rA = rf[:, 0:2048].rearrange("p (a t) -> p a t", a=HT)
        w2rA = rf[:, 2048:4096].rearrange("p (a t) -> p a t", a=HT)
        w1rB = rf[:, 4096:6144].rearrange("p (a t) -> p a t", a=HT)
        w2rB = rf[:, 6144:8192].rearrange("p (a t) -> p a t", a=HT)
        sgf0 = rf[:, 8192:8704]
        sgf1 = rf[:, 8704:9216]
        w3r = rf[:, 0:5632].rearrange("p (a t) -> p a t", a=FT)
        mb_sb = rf[:, 5632:6656]
        rb_sb = rf[:, 6656:7680]
        y2t = rf[:, 7680:8704]
        sums_r = rf[:, 2048:3072]
        ssq_r = rf[:, 3072:4096]
        ln2a = rf[:, 0:1024]
        ln2b = rf[:, 1024:2048]

        ps = ctx.enter_context(nc.psum_tensor("ps", [128, 4096], F32))

        def bank(i):
            return ps[:, i * 512:(i + 1) * 512]

        psbf = ps[:, 3584:4096].bitcast(BF16)

        g_init = Sem(ctx, nc, "g_init")
        g_pre = Sem(ctx, nc, "g_pre")
        g_preB = Sem(ctx, nc, "g_preB")
        g_vb0 = Sem(ctx, nc, "g_vb0")
        g_vb1 = Sem(ctx, nc, "g_vb1")
        g_wogA = Sem(ctx, nc, "g_wogA")
        g_wogB = Sem(ctx, nc, "g_wogB")
        g_xr0 = Sem(ctx, nc, "g_xr0")
        g_xr1 = Sem(ctx, nc, "g_xr1")
        s_const = Sem(ctx, nc, "s_const")
        s_gam0 = Sem(ctx, nc, "s_gam0")
        s_gam1 = Sem(ctx, nc, "s_gam1")
        s_w12A = Sem(ctx, nc, "s_w12A")
        s_w12B = Sem(ctx, nc, "s_w12B")
        s_w3 = Sem(ctx, nc, "s_w3")
        o_dma = Sem(ctx, nc, "o_dma")
        d_dbg = Sem(ctx, nc, "d_dbg")
        pe = Sem(ctx, nc, "pe")
        ve = Sem(ctx, nc, "ve")
        sc = Sem(ctx, nc, "sc")
        gpc = Sem(ctx, nc, "gpc")
        ck = {}

        blk = ctx.enter_context(nc.Block())

        gjobs = []
        gcols = []
        for m, srcd in enumerate((d_w1, d_w2, d_w3)):
            nb, _, fdim = srcd.shape
            c0 = len(gjobs)
            for a in range(nb):
                off = 0
                while off < fdim:
                    w = min(2048, fdim - off)
                    gjobs.append((m, srcd[a, :, off:off + w], w))
                    off += w
            gcols.append((c0, len(gjobs)))
        NGALL = len(gjobs)
        assert NGALL <= 288, NGALL
        gamma_n = [0]

        def emit_gamma(k):
            for _ in range(k):
                n = gamma_n[0]
                if n >= NGALL:
                    return
                m, gsrc, w = gjobs[n]
                buf = (gam0 if n % 2 == 0 else gam1)[:, 0:w]
                gsem = s_gam0 if n % 2 == 0 else s_gam1
                col = gpart[:, n:n + 1]

                def gdma(sy, n=n, gsrc=gsrc, buf=buf, gsem=gsem):
                    if n >= 2:
                        sy.wait_ge(sc.h, ck[f"g_red{n - 2}"])
                    else:
                        sy.wait_ge(pe.h, ck[f"pe_pre_state{NPC - 1}"])
                    gsem.dma(sy.dma_start(buf, gsrc))
                    ck[f"g_ld{n}"] = gsem.v

                blk.sync(gdma)

                def gred(s, n=n, buf=buf, col=col, w=w, gsem=gsem):
                    s.wait_ge(gsem.h, ck[f"g_ld{n}"])
                    if n == 0:
                        s.wait_ge(pe.h, ck[f"pe_pre_state{NPC - 1}"])
                    s.wait_ge(sc.h, sc.v)
                    sc.inc(s.activation(junk_b[:, 0:w], buf, AF.Abs, accum_out=col))
                    ck[f"g_red{n}"] = sc.v

                blk.scalar(gred)
                gamma_n[0] += 1

        def ph0_gp(gp):
            g_init.dma(gp.dma_start(wq_sb[:], d_wq.rearrange("(a p) k -> p a k", p=128)))
            g_init.dma(gp.dma_start(wk_sb[:], d_wk.rearrange("(a p) k -> p a k", p=128)))
            g_init.dma(gp.dma_start(gw_b[:], bass.AP(d_gw.tensor, 0, [[0, 128], [1, H]])))
            for i in range(4):
                g_init.dma(gp.dma_start(
                    xT[:, i * 4:(i + 1) * 4, :],
                    d_xT[i * 512:(i + 1) * 512, :].rearrange("(a p) t -> p a t", p=128)))
            for i in range(4):
                g_init.dma(gp.dma_start(
                    xtok[:, i * 2:(i + 1) * 2, :],
                    d_x[i * 256:(i + 1) * 256, :].rearrange("(a p) t -> p a t", p=128)))
            ck["gp_init_dma"] = g_init.v
            gp.memset(ones_cb[:], 1.0)
            gp.memset(ones_cf[:], 1.0)
            gp.memset(ones_rf[:], 1.0)
            gp.memset(cst[:, 0:1], EPS)
            gp.memset(cst[:, 1:2], 0.5e-8)
            gp.memset(cst[:, 2:3], -0.5e-8)
            gpc.inc(gp.memset(S_sb, 0.0))
            ck["gp_init"] = gpc.v

        blk.gpsimd(ph0_gp)

        def ph0_sync(sy):
            s_const.dma(sy.dma_start(id_sb[:], d_id))
            s_const.dma(sy.dma_start(mask_sb[:], d_mask))
            ck["sy_consts"] = s_const.v

        blk.sync(ph0_sync)

        def ph2_pe(te):
            te.wait_ge(g_init.h, ck["gp_init_dma"])
            for half in range(2):
                for hh in range(HT):
                    li = te.matmul(bank(half)[:, 0:512], wq_sb[:, hh, :],
                                   xT[:, hh, half * 512:(half + 1) * 512],
                                   start=(hh == 0), stop=(hh == HT - 1))
                pe.inc(li)
                ck[f"pe_q{half}"] = pe.v
            for half in range(2):
                for hh in range(HT):
                    li = te.matmul(bank(2 + half)[:, 0:512], wk_sb[:, hh, :],
                                   xT[:, hh, half * 512:(half + 1) * 512],
                                   start=(hh == 0), stop=(hh == HT - 1))
                pe.inc(li)
                ck[f"pe_k{half}"] = pe.v

        blk.tensor(ph2_pe)

        def ph2_sc(s):
            for half in range(2):
                s.wait_ge(pe.h, ck[f"pe_q{half}"])
                sc.inc(s.activation(siluq[:, half * 512:(half + 1) * 512],
                                    bank(half)[:, 0:512], AF.Silu))
            ck["sc_siluq"] = sc.v
            for half in range(2):
                s.wait_ge(pe.h, ck[f"pe_k{half}"])
                sc.inc(s.activation(ksig[:, half * 512:(half + 1) * 512],
                                    bank(2 + half)[:, 0:512], AF.Sigmoid))
            ck["sc_ksig"] = sc.v
            s.wait_ge(sc.h, sc.v)
            sc.inc(s.activation(E, ksig, AF.Sigmoid, scale=-1.0))
            s.wait_ge(sc.h, sc.v)
            sc.inc(s.activation(ccum, E, AF.Ln))
            ck["sc_sp"] = sc.v

        blk.scalar(ph2_sc)

        def ph2_ve(v):
            v.wait_ge(sc.h, ck["sc_sp"])
            for c in range(NCH):
                sl = slice(c * C, (c + 1) * C)
                li = v.tensor_tensor_scan(sp[:, sl], ccum[:, sl], ccum[:, sl], 0.0, ALU.add, ALU.bypass)
            ve.inc(li)
            ck["ve_scan"] = ve.v

        blk.vector(ph2_ve)

        def ph2_sc2(s):
            s.wait_ge(ve.h, ck["ve_scan"])
            sc.inc(s.activation(E, sp, AF.Exp, scale=1.0))
            sc.inc(s.activation(Einv, sp, AF.Exp, scale=-1.0))
            ck["sc_E"] = sc.v

        blk.scalar(ph2_sc2)

        def ph2_ve2(v):
            v.wait_ge(sc.h, ck["sc_E"])
            ve.inc(v.scalar_tensor_tensor(qp, siluq, SCALE, E, ALU.mult, ALU.mult))
            ve.inc(v.tensor_mul(kp, ksig, Einv))
            v.wait_ge(ve.h, ve.v)
            for c in range(NCH):
                sl = slice(c * C, (c + 1) * C)
                li = v.tensor_scalar(kpp[:, sl], kp[:, sl],
                                     E[:, c * C + C - 1:c * C + C], None, ALU.mult)
            ve.inc(li)
            ck["ve_prep"] = ve.v

        blk.vector(ph2_ve2)

        for pcn in range(NPC):
            def pre_gp(gp, pcn=pcn):
                vpre = vpreA if pcn % 2 == 0 else vpreB
                xpreT = xpreTA if pcn % 2 == 0 else xpreTB
                psem = g_pre if pcn % 2 == 0 else g_preB
                if pcn >= 2:
                    gp.wait_ge(pe.h, ck[f"pe_pre_state{pcn - 2}"])
                psem.dma(gp.dma_start(
                    vpre[:, :, :],
                    d_xp[pcn * PC:(pcn + 1) * PC, :].rearrange("(a p) t -> p a t", p=128)))
                psem.dma(gp.dma_start(
                    xpreT[:, :, :],
                    d_xpT[:, pcn * PC:(pcn + 1) * PC].rearrange("(a p) t -> p a t", p=128)))
                ck[f"gp_pre{pcn}"] = (psem.h, psem.v)

            blk.gpsimd(pre_gp)

            def pre_pe1(te, pcn=pcn):
                xpreT = xpreTA if pcn % 2 == 0 else xpreTB
                psh, psv = ck[f"gp_pre{pcn}"]
                te.wait_ge(psh, psv)
                if pcn == 0:
                    te.wait_ge(g_init.h, ck["gp_init_dma"])
                if pcn == 0:
                    te.wait_ge(sc.h, ck["sc_siluq"])
                else:
                    te.wait_ge(sc.h, ck[f"sc_pre_ks{pcn - 1}"])
                for hh in range(HT):
                    li = te.matmul(bank(0)[:, 0:512], wk_sb[:, hh, :], xpreT[:, hh, :],
                                   start=(hh == 0), stop=(hh == HT - 1))
                pe.inc(li)
                ck[f"pe_pre_k{pcn}"] = pe.v

            blk.tensor(pre_pe1)

            def pre_sc1(s, pcn=pcn):
                s.wait_ge(pe.h, ck[f"pe_pre_k{pcn}"])
                sc.inc(s.activation(kspre, bank(0)[:, 0:512], AF.Sigmoid))
                ck[f"sc_pre_ks{pcn}"] = sc.v
                s.wait_ge(sc.h, sc.v)
                sc.inc(s.activation(kfac, kspre, AF.Sigmoid, scale=-1.0))
                s.wait_ge(sc.h, sc.v)
                sc.inc(s.activation(sppre, kfac, AF.Ln))
                ck[f"sc_pre_sp{pcn}"] = sc.v

            blk.scalar(pre_sc1)

            def pre_ve1(v, pcn=pcn):
                v.wait_ge(sc.h, ck[f"sc_pre_sp{pcn}"])
                ve.inc(v.tensor_tensor_scan(ccpre, sppre, sppre, 0.0, ALU.add, ALU.bypass))
                ck[f"ve_pre_scan{pcn}"] = ve.v

            blk.vector(pre_ve1)

            def pre_sc2a(s, pcn=pcn):
                s.wait_ge(ve.h, ck[f"ve_pre_scan{pcn}"])
                sc.inc(s.activation(ncol, ccpre[:, PC - 1:PC], AF.Copy))
                sc.inc(s.activation(dcol, ccpre[:, PC - 1:PC], AF.Exp))
                ck[f"sc_pre_nc{pcn}"] = sc.v

            blk.scalar(pre_sc2a)

            def pre_ping(v, pcn=pcn):
                v.wait_ge(sc.h, ck[f"sc_pre_nc{pcn}"])
                ve.inc(v.nop())
                ck[f"ve_pre_ping{pcn}"] = ve.v

            blk.vector(pre_ping)

            def pre_sc2b(s, pcn=pcn):
                s.wait_ge(ve.h, ck[f"ve_pre_ping{pcn}"])
                sc.inc(s.activation(kfac, ccpre, AF.Exp, scale=-1.0, bias=ncol))
                ck[f"sc_pre_fac{pcn}"] = sc.v

            blk.scalar(pre_sc2b)

            def pre_ve2(v, pcn=pcn):
                v.wait_ge(sc.h, ck[f"sc_pre_fac{pcn}"])
                if pcn >= 1:
                    v.wait_ge(pe.h, ck[f"pe_pre_kT{pcn - 1}"])
                ve.inc(v.tensor_mul(kppre, kspre, kfac))
                ck[f"ve_pre_kpp{pcn}"] = ve.v

            blk.vector(pre_ve2)

            def pre_pe2(te, pcn=pcn):
                if pcn == 0:
                    te.wait_ge(s_const.h, ck["sy_consts"])
                te.wait_ge(ve.h, ck[f"ve_pre_kpp{pcn}"])
                if pcn >= 1:
                    te.wait_ge(ve.h, ck[f"ve_pre_kT{pcn - 1}"])
                for sbk in range(4):
                    li = te.transpose(psbf[:, sbk * 128:(sbk + 1) * 128],
                                      kppre[:, sbk * 128:(sbk + 1) * 128], id_sb[:])
                pe.inc(li)
                ck[f"pe_pre_kT{pcn}"] = pe.v

            blk.tensor(pre_pe2)

            def pre_ve3(v, pcn=pcn):
                v.wait_ge(pe.h, ck[f"pe_pre_kT{pcn}"])
                if pcn >= 1:
                    v.wait_ge(pe.h, ck[f"pe_pre_state{pcn - 1}"])
                ve.inc(v.tensor_copy(rbig[:, 42176:42688], psbf[:, 0:512]))
                ck[f"ve_pre_kT{pcn}"] = ve.v

            blk.vector(pre_ve3)

            for q in range(4):
                def pre_pe3(te, pcn=pcn, q=q):
                    if q == 0:
                        te.wait_ge(ve.h, ck[f"ve_pre_kT{pcn}"])
                        if pcn == 0:
                            te.wait_ge(sc.h, ck["sc_ksig"])
                            te.wait_ge(gpc.h, ck["gp_init"])
                    bq = bank(2 + (q % 2))
                    if q >= 2:
                        te.wait_ge(ve.h, ck[f"ve_pre_stt{pcn}_{q - 2}"])
                    elif pcn >= 1:
                        te.wait_ge(ve.h, ck[f"ve_pre_stt{pcn - 1}_{q + 2}"])
                    vpre = vpreA if pcn % 2 == 0 else vpreB
                    for sbk in range(4):
                        li = te.matmul(bq[:, 0:512], kppTp[:, sbk, :],
                                       vpre[:, sbk, q * 512:(q + 1) * 512],
                                       start=(sbk == 0), stop=(sbk == 3))
                    pe.inc(li)
                    ck[f"pe_pre_st{pcn}_{q}"] = pe.v
                    if q == 3:
                        ck[f"pe_pre_state{pcn}"] = pe.v

                blk.tensor(pre_pe3)

                def pre_ve4(v, pcn=pcn, q=q):
                    v.wait_ge(pe.h, ck[f"pe_pre_st{pcn}_{q}"])
                    ve.inc(v.scalar_tensor_tensor(
                        S_sb[:, q * 512:(q + 1) * 512], S_sb[:, q * 512:(q + 1) * 512],
                        dcol, bank(2 + (q % 2))[:, 0:512], ALU.mult, ALU.add))
                    ck[f"ve_pre_stt{pcn}_{q}"] = ve.v

                blk.vector(pre_ve4)

        ck["ve_prefix_done"] = ve.v

        if _DEBUG_TAPS:
            def tap_pre(gp):
                gp.wait_ge(ve.h, ck["ve_prefix_done"])
                gp.wait_ge(ve.h, ck["ve_prep"])
                d_dbg.dma(gp.dma_start(dbg["S"], S_sb.rearrange("p (a t) -> p a t", a=1)[:, 0, :]))
                d_dbg.dma(gp.dma_start(dbg["ksig"], ksig))
                d_dbg.dma(gp.dma_start(dbg["siluq"], siluq))
                ck["dbg_pre"] = d_dbg.v
            blk.gpsimd(tap_pre)

        emit_gamma(10 ** 9)

        for c in range(NCH):
            par = c % 2
            pr = c // 2
            cs = slice(c * C, (c + 1) * C)
            cend = E[:, c * C + C - 1:c * C + C]

            if par == 1:
                def gla_gp(gp, c=c):
                    vb = vbuf0 if (c // 2) % 2 == 0 else vbuf1
                    vs = g_vb0 if (c // 2) % 2 == 0 else g_vb1
                    if c >= 5:
                        gp.wait_ge(pe.h, ck[f"pe_oq3_{c - 4}"])
                    gp.wait_ge(g_init.h, ck["gp_init_dma"])
                    vs.dma(gp.dma_start(vb[0:64, :], xtok[64:128, c // 2, :]))
                    ck[f"gp_vb{c}"] = (vs.h, vs.v)

                blk.gpsimd(gla_gp)

            def gla_pe_a(te, c=c, cs=cs):
                if c == 0:
                    te.wait_ge(ve.h, ck["ve_prep"])
                    te.wait_ge(ve.h, ck["ve_prefix_done"])
                    te.wait_ge(s_const.h, ck["sy_consts"])
                    if _DEBUG_TAPS:
                        te.wait_ge(d_dbg.h, ck["dbg_pre"])
                if c >= 1:
                    te.wait_ge(ve.h, ck[f"ve_Pevac{c - 1}"])
                    te.wait_ge(ve.h, ck[f"ve_kTevac{c - 1}"])
                pe.inc(te.matmul(bank(6)[0:64, 0:64], kp[:, cs], qp[:, cs], start=True, stop=True))
                ck[f"pe_P{c}"] = pe.v
                pe.inc(te.transpose(psbf[0:64, 512:640], kpp[:, cs], id_sb[:]))
                ck[f"pe_kT{c}"] = pe.v

            blk.tensor(gla_pe_a)

            def gla_ve_a(v, c=c):
                v.wait_ge(pe.h, ck[f"pe_P{c}"])
                if c >= 1:
                    v.wait_ge(pe.h, ck[f"pe_oq3_{c - 1}"])
                ve.inc(v.tensor_mul(P_sb[0:64, :], bank(6)[0:64, 0:64], mask_sb[:]))
                ck[f"ve_Pevac{c}"] = ve.v
                v.wait_ge(pe.h, ck[f"pe_kT{c}"])
                ve.inc(v.tensor_copy(kppT_sb[0:64, :], psbf[0:64, 512:640]))
                ck[f"ve_kTevac{c}"] = ve.v

            blk.vector(gla_ve_a)

            for q in range(4):
                def gla_pe_b(te, c=c, par=par, pr=pr, cs=cs, q=q):
                    vb = ((vbuf0 if (c // 2) % 2 == 0 else vbuf1)[0:64, :] if par
                          else xtok[0:64, c // 2, :])
                    P_l = P_sb[0:64, :]
                    kT_l = kppT_sb[0:64, :]
                    tp = (0, 64) if par == 1 else None
                    orow = slice(64, 128) if par == 1 else slice(0, 64)
                    if q == 0:
                        te.wait_ge(ve.h, ck[f"ve_kTevac{c}"])
                        te.wait_ge(ve.h, ck[f"ve_Pevac{c}"])
                        if par == 1:
                            vsh, vsv = ck[f"gp_vb{c}"]
                            te.wait_ge(vsh, vsv)
                        if par == 0 and pr >= 1:
                            te.wait_ge(ve.h, ck[f"ve_oevac{pr - 1}"])
                    sbank = bank(4 + (q % 2))
                    if q >= 2:
                        te.wait_ge(ve.h, ck[f"ve_stt{c}_{q - 2}"])
                    elif c >= 1:
                        te.wait_ge(ve.h, ck[f"ve_stt{c - 1}_{q + 2}"])
                    te.matmul(sbank[:, 0:512], kT_l, vb[:, q * 512:(q + 1) * 512],
                              start=True, stop=True)
                    te.matmul(bank(q)[orow, 0:512], P_l, vb[:, q * 512:(q + 1) * 512],
                              start=True, stop=False, tile_position=tp)
                    pe.inc(te.matmul(bank(q)[orow, 0:512], qp[:, cs], S_sb[:, q * 512:(q + 1) * 512],
                                     start=False, stop=True, tile_position=tp))
                    ck[f"pe_oq{q}_{c}"] = pe.v
                    if q == 3:
                        ck[f"pe_oq3_{c}"] = pe.v

                blk.tensor(gla_pe_b)

                def gla_ve_b(v, c=c, cend=cend, q=q):
                    v.wait_ge(pe.h, ck[f"pe_oq{q}_{c}"])
                    ve.inc(v.scalar_tensor_tensor(
                        S_sb[:, q * 512:(q + 1) * 512], S_sb[:, q * 512:(q + 1) * 512],
                        cend, bank(4 + (q % 2))[:, 0:512], ALU.mult, ALU.add))
                    ck[f"ve_stt{c}_{q}"] = ve.v

                blk.vector(gla_ve_b)

            if par == 1:
                def gla_ve_ev(v, c=c, pr=pr):
                    v.wait_ge(pe.h, ck[f"pe_oq3_{c}"])
                    for q in range(4):
                        li = v.tensor_copy(o_tok[:, pr, q * 512:(q + 1) * 512], bank(q)[:, 0:512])
                    ve.inc(li)
                    ck[f"ve_oevac{pr}"] = ve.v

                blk.vector(gla_ve_ev)

        ck["gla_done_ve"] = ve.v

        if _DEBUG_TAPS:
            def tap_o(gp):
                gp.wait_ge(ve.h, ck["gla_done_ve"])
                for tt in range(TT):
                    d_dbg.dma(gp.dma_start(dbg["o"][tt * 128:(tt + 1) * 128, :], o_tok[:, tt, :]))
                ck["dbg_o"] = d_dbg.v
            blk.gpsimd(tap_o)

        def ph5_sc(s):
            for tt in range(TT):
                s.wait_ge(ve.h, ck[f"ve_oevac{tt}"])
                s.wait_ge(sc.h, sc.v)
                sc.inc(s.activation(junk_b[:, 0:2048], o_tok[:, tt, :], AF.Square,
                                    accum_out=oss[:, tt:tt + 1]))
            s.wait_ge(sc.h, sc.v)
            sc.inc(s.activation(lnt[:, 0:8], oss[:, 0:8], AF.Ln, scale=1.0 / H, bias=cst[:, 0:1]))
            s.wait_ge(sc.h, sc.v)
            sc.inc(s.activation(rstdo[:, 0:8], lnt[:, 0:8], AF.Exp, scale=-0.5))
            ck["sc_rstdo"] = sc.v

        blk.scalar(ph5_sc)

        ntr = 0
        for pr in range(TT):
            for vg in range(2):
                def ph5_pe(te, pr=pr, vg=vg, ntr=ntr):
                    te.wait_ge(ve.h, ck[f"ve_oevac{pr}"])
                    if ntr == 0:
                        te.wait_ge(ve.h, ck[f"ve_kTevac{NCH - 1}"])
                    else:
                        te.wait_ge(ve.h, ck[f"ve_oT{ntr - 1}"])
                    for j in range(8):
                        vt = vg * 8 + j
                        li = te.transpose(psbf[:, j * 128:(j + 1) * 128],
                                          o_tok[:, pr, vt * 128:(vt + 1) * 128], id_sb[:])
                    pe.inc(li)
                    ck[f"pe_oT{ntr}"] = pe.v

                blk.tensor(ph5_pe)

                def ph5_ve(v, pr=pr, vg=vg, ntr=ntr):
                    v.wait_ge(pe.h, ck[f"pe_oT{ntr}"])
                    for j in range(8):
                        vt = vg * 8 + j
                        li = v.tensor_copy(oTv[:, vt, pr * 128:(pr + 1) * 128],
                                           psbf[:, j * 128:(j + 1) * 128])
                    ve.inc(li)
                    ck[f"ve_oT{ntr}"] = ve.v

                blk.vector(ph5_ve)
                ntr += 1

        ck["ve_oT_done"] = ve.v

        gseq = 0
        p6_lastuse = {}
        for os_i in range(NOS):
            osl = slice(os_i * OS, (os_i + 1) * OS)
            wb, gb = (wogA, wigA) if os_i % 2 == 0 else (wogB, wigB)

            def ph6_gp(gp, os_i=os_i, wb=wb, gb=gb):
                ws = g_wogA if os_i % 2 == 0 else g_wogB
                if os_i == 1:
                    gp.wait_ge(ve.h, ck["gla_done_ve"])
                if os_i >= 2:
                    gp.wait_ge(pe.h, ck[f"pe_go{os_i - 2}_{TT - 1}"])
                ws.dma(gp.dma_start(wb.rearrange("p a t -> p (a t)"), d_wog[os_i, :, :]))
                ws.dma(gp.dma_start(gb.rearrange("p a t -> p (a t)"), d_wig[os_i, :, :]))
                ck[f"gp_wog{os_i}"] = (ws.h, ws.v)

            blk.gpsimd(ph6_gp)

            for tt in range(TT):
                def ph6_pe(te, os_i=os_i, tt=tt, wb=wb, gb=gb):
                    if tt == 0:
                        wsh, wsv = ck[f"gp_wog{os_i}"]
                        te.wait_ge(wsh, wsv)
                        if os_i == 0:
                            te.wait_ge(ve.h, ck["ve_oT_done"])
                    prev = p6_lastuse.get(tt % 6)
                    if prev is not None:
                        te.wait_ge(sc.h, ck[f"sc_sg{prev[0]}_{prev[1]}"])
                    p6_lastuse[tt % 6] = (os_i, tt)
                    bk = bank(tt % 6)
                    for vt in range(HT):
                        te.matmul(bk[:, 0:OS], oTv[:, vt, tt * 128:(tt + 1) * 128],
                                  wb[:, vt, :], start=(vt == 0), stop=False)
                        li = te.matmul(bk[:, 0:OS], xT[:, vt, tt * 128:(tt + 1) * 128],
                                       gb[:, vt, :], start=False, stop=(vt == HT - 1))
                    pe.inc(li)
                    ck[f"pe_go{os_i}_{tt}"] = pe.v

                blk.tensor(ph6_pe)

                def ph6_sc(s, os_i=os_i, tt=tt):
                    s.wait_ge(pe.h, ck[f"pe_go{os_i}_{tt}"])
                    cur = os_i * TT + tt
                    if cur >= 2:
                        po, pt = divmod(cur - 2, TT)
                        s.wait_ge(ve.h, ck[f"ve_ep{po}_{pt}"])
                    sgb = sg0 if tt % 2 == 0 else sg1
                    sc.inc(s.activation(sgb, bank(tt % 6)[:, 0:OS], AF.Silu))
                    ck[f"sc_sg{os_i}_{tt}"] = sc.v

                blk.scalar(ph6_sc)

                def ph6_ve(v, os_i=os_i, tt=tt, osl=osl):
                    v.wait_ge(sc.h, ck[f"sc_sg{os_i}_{tt}"])
                    if _DEBUG_TAPS and os_i == 0 and tt == 0:
                        v.wait_ge(d_dbg.h, ck["dbg_o"])
                    sgb = sg0 if tt % 2 == 0 else sg1
                    v.wait_ge(ve.h, ve.v)
                    ve.inc(v.scalar_tensor_tensor(og6, o_tok[:, tt, osl], rstdo[:, tt:tt + 1],
                                                  gw_b[:, osl], ALU.mult, ALU.mult))
                    v.wait_ge(ve.h, ve.v)
                    ve.inc(v.scalar_tensor_tensor(
                        o_tok[:, tt, osl], sgb, 1.0, og6, ALU.mult, ALU.mult))
                    ck[f"ve_ep{os_i}_{tt}"] = ve.v

                blk.vector(ph6_ve)

        for tt in range(TT):
            def ph6b_gp(gp, tt=tt):
                xr = xres0 if tt % 2 == 0 else xres1
                xsem = g_xr0 if tt % 2 == 0 else g_xr1
                if tt < 2:
                    gp.wait_ge(ve.h, ck["ve_prep"])
                else:
                    gp.wait_ge(ve.h, ck[f"ve_z{tt - 2}"])
                xsem.dma(gp.dma_start(xr, d_x[tt * 128:(tt + 1) * 128, :]))
                ck[f"gp_xres{tt}"] = (xsem.h, xsem.v)

            blk.gpsimd(ph6b_gp)

            def ph6b_ve(v, tt=tt):
                xr = xres0 if tt % 2 == 0 else xres1
                xsh, xsv = ck[f"gp_xres{tt}"]
                v.wait_ge(xsh, xsv)
                v.wait_ge(ve.h, ck[f"ve_ep{NOS - 1}_{tt}"])
                ve.inc(v.scalar_tensor_tensor(
                    o_tok[:, tt, :], xr, 1.0, o_tok[:, tt, :], ALU.mult, ALU.add,
                    accum_out=sum_c[:, tt:tt + 1]))
                ck[f"ve_z{tt}"] = ve.v

            blk.vector(ph6b_ve)

            def ph6b_sc(s, tt=tt):
                s.wait_ge(ve.h, ck[f"ve_z{tt}"])
                s.wait_ge(sc.h, sc.v)
                sc.inc(s.activation(junk_b[:, 0:2048], o_tok[:, tt, :], AF.Square,
                                    accum_out=sq_c[:, tt:tt + 1]))
                ck[f"sc_zsq{tt}"] = sc.v

            blk.scalar(ph6b_sc)

        def ph6b_sc2a(s):
            s.wait_ge(sc.h, ck[f"sc_zsq{TT - 1}"])
            sc.inc(s.activation(mean_c[:, 0:TT], sum_c[:, 0:TT], AF.Copy, scale=1.0 / H))
            sc.inc(s.activation(e2_c[:, 0:TT], sq_c[:, 0:TT], AF.Copy, scale=1.0 / H))
            s.wait_ge(sc.h, sc.v)
            sc.inc(s.activation(m2_c[:, 0:TT], mean_c[:, 0:TT], AF.Square))
            s.wait_ge(sc.h, sc.v)
            sc.inc(s.activation(nm2_c[:, 0:TT], m2_c[:, 0:TT], AF.Copy, scale=-1.0, bias=EPS))
            ck["sc_ln1a"] = sc.v

        blk.scalar(ph6b_sc2a)

        def ph6b_ping(v):
            v.wait_ge(sc.h, ck["sc_ln1a"])
            ve.inc(v.nop())
            ck["ve_ln1ping"] = ve.v

        blk.vector(ph6b_ping)

        def ph6b_sc2b(s):
            s.wait_ge(ve.h, ck["ve_ln1ping"])
            for tt in range(TT):
                sc.inc(s.activation(lnt[:, tt:tt + 1], e2_c[:, tt:tt + 1], AF.Ln,
                                    bias=nm2_c[:, tt:tt + 1]))
            s.wait_ge(sc.h, sc.v)
            sc.inc(s.activation(r_c[:, 0:TT], lnt[:, 0:TT], AF.Exp, scale=-0.5))
            ck["sc_ln1"] = sc.v

        blk.scalar(ph6b_sc2b)

        def ph6b_ve2(v):
            v.wait_ge(sc.h, ck["sc_ln1"])
            for tt in range(TT):
                li = v.tensor_scalar(o_tok[:, tt, :], o_tok[:, tt, :],
                                     mean_c[:, tt:tt + 1], r_c[:, tt:tt + 1],
                                     ALU.subtract, ALU.mult)
            ve.inc(li)
            ck["ve_y1"] = ve.v

        blk.vector(ph6b_ve2)

        if _DEBUG_TAPS:
            def tap_y1(gp):
                gp.wait_ge(ve.h, ck["ve_y1"])
                for tt in range(TT):
                    d_dbg.dma(gp.dma_start(dbg["y1"][tt * 128:(tt + 1) * 128, :], o_tok[:, tt, :]))
                ck["dbg_y1"] = d_dbg.v
            blk.gpsimd(tap_y1)

        def gfin_pe(te):
            te.wait_ge(sc.h, ck[f"g_red{NGALL - 1}"])
            te.wait_ge(sc.h, ck[f"sc_sg{NOS - 1}_{TT - 1}"])
            pe.inc(te.matmul(bank(4)[0:1, 0:NGALL], ones_cf[:], gpart[:, 0:NGALL], start=True, stop=True))
            ck["pe_gsum"] = pe.v

        blk.tensor(gfin_pe)

        def gfin_ve(v):
            v.wait_ge(pe.h, ck["pe_gsum"])
            for m in range(3):
                a0, a1 = gcols[m]
                li = v.tensor_reduce(grow[0:1, m:m + 1], bank(4)[0:1, a0:a1], AX.X, ALU.add)
            ve.inc(li)
            ck["ve_gsum"] = ve.v

        blk.vector(gfin_ve)

        def gfin_sc(s):
            s.wait_ge(ve.h, ck["ve_gsum"])
            sc.inc(s.activation(throw[0:1, 0:3], grow[0:1, 0:3], AF.Copy,
                                scale=0.5 / QCNT, bias=0.5e-8))
            sc.inc(s.activation(nthrow[0:1, 0:3], grow[0:1, 0:3], AF.Copy,
                                scale=-0.5 / QCNT, bias=-0.5e-8))
            ck["sc_th"] = sc.v

        blk.scalar(gfin_sc)

        def gfin_pe2(te):
            te.wait_ge(sc.h, ck["sc_th"])
            te.matmul(bank(4)[:, 0:3], ones_rf[:], throw[0:1, 0:3], start=True, stop=True)
            pe.inc(te.matmul(bank(4)[:, 16:19], ones_rf[:], nthrow[0:1, 0:3], start=True, stop=True))
            ck["pe_thb"] = pe.v

        blk.tensor(gfin_pe2)

        def gfin_sc2(s):
            s.wait_ge(pe.h, ck["pe_thb"])
            sc.inc(s.copy(th_c[:, 0:3], bank(4)[:, 0:3]))
            sc.inc(s.copy(nth_c[:, 0:3], bank(4)[:, 16:19]))
            ck["sc_thc"] = sc.v

        blk.scalar(gfin_sc2)

        def gfin_ve2(v):
            v.wait_ge(sc.h, ck["sc_thc"])
            ve.inc(v.tensor_copy(og6[:, 0:1], ones_cb[:]))
            ck["ve_th"] = ve.v

        blk.vector(gfin_ve2)

        if _DEBUG_TAPS:
            blk.gpsimd(lambda gp: (
                gp.wait_ge(ve.h, ck["ve_th"]),
                d_dbg.dma(gp.dma_start(dbg["th"][:, 0:3], th_c[:, 0:3])),
                d_dbg.dma(gp.dma_start(dbg["th"][:, 4:7], nth_c[:, 0:3])),
                ck.__setitem__("dbg_th", d_dbg.v)))

        ntr = 0
        for pr in range(TT):
            for vg in range(2):
                def ph7_pe(te, pr=pr, vg=vg, ntr=ntr):
                    if ntr == 0:
                        te.wait_ge(ve.h, ck["ve_y1"])
                        te.wait_ge(ve.h, ck["ve_th"])
                        te.wait_ge(sc.h, ck["sc_ln1"])
                    else:
                        te.wait_ge(ve.h, ck[f"ve_y1T{ntr - 1}"])
                    for j in range(8):
                        vt = vg * 8 + j
                        li = te.transpose(psbf[:, j * 128:(j + 1) * 128],
                                          o_tok[:, pr, vt * 128:(vt + 1) * 128], id_sb[:])
                    pe.inc(li)
                    ck[f"pe_y1T{ntr}"] = pe.v

                blk.tensor(ph7_pe)

                def ph7_ve(v, pr=pr, vg=vg, ntr=ntr):
                    v.wait_ge(pe.h, ck[f"pe_y1T{ntr}"])
                    for j in range(8):
                        vt = vg * 8 + j
                        li = v.tensor_copy(y1T[:, vt, pr * 128:(pr + 1) * 128],
                                           psbf[:, j * 128:(j + 1) * 128])
                    ve.inc(li)
                    ck[f"ve_y1T{ntr}"] = ve.v

                blk.vector(ph7_ve)
                ntr += 1

        ck["ve_y1T_done"] = ve.v

        p8_lastuse = {}
        for ft in range(FT):
            wr1, wr2 = (w1rA, w2rA) if ft % 2 == 0 else (w1rB, w2rB)

            def ph8_sy(sy, ft=ft, wr1=wr1, wr2=wr2):
                wsem = s_w12A if ft % 2 == 0 else s_w12B
                if ft < 2:
                    sy.wait_ge(ve.h, ck["ve_y1T_done"])
                else:
                    sy.wait_ge(ve.h, ck[f"ve_q2_{ft - 2}"])
                wsem.dma(sy.dma_start(wr1.rearrange("p a t -> p (a t)"), d_w1[ft, :, :]))
                wsem.dma(sy.dma_start(wr2.rearrange("p a t -> p (a t)"), d_w2[ft, :, :]))
                ck[f"sy_w12_{ft}"] = (wsem.h, wsem.v)

            blk.sync(ph8_sy)

            def quant(v, wraw, wqd, m):
                srcf = wraw.rearrange("p a t -> p (a t)")
                dst = wqd.rearrange("p a t -> p (a t)")
                v.wait_ge(ve.h, ve.v)
                ve.inc(v.tensor_single_scalar(dst, srcf, nth_c[:, m:m + 1], ALU.is_le))
                ve.inc(v.tensor_single_scalar(pq8, srcf, th_c[:, m:m + 1], ALU.is_ge))
                v.wait_ge(ve.h, ve.v)
                ve.inc(v.tensor_sub(dst, pq8, dst))

            def ph8_ve_q1(v, ft=ft, wr1=wr1):
                wq1_sb = wq1A if ft % 2 == 0 else wq1B
                wsh, wsv = ck[f"sy_w12_{ft}"]
                v.wait_ge(wsh, wsv)
                if ft == 0:
                    v.wait_ge(ve.h, ck["ve_th"])
                    v.wait_ge(ve.h, ck["ve_y1T_done"])
                    if _DEBUG_TAPS:
                        v.wait_ge(d_dbg.h, ck["dbg_th"])
                if ft >= 2:
                    v.wait_ge(pe.h, ck[f"pe_x1_{ft - 2}_1"])
                quant(v, wr1, wq1_sb, 0)
                ck[f"ve_q1_{ft}"] = ve.v

            blk.vector(ph8_ve_q1)

            def ph8_pe_x1(te, ft=ft):
                wq1_sb = wq1A if ft % 2 == 0 else wq1B
                te.wait_ge(ve.h, ck[f"ve_q1_{ft}"])
                for half in range(2):
                    bkid = (4 * ft + half) % 7
                    prev = p8_lastuse.get(bkid)
                    if prev is not None:
                        te.wait_ge(prev[0], prev[1])
                    for hh in range(HT):
                        li = te.matmul(bank(bkid)[:, 0:512], wq1_sb[:, hh, :],
                                       y1T[:, hh, half * 512:(half + 1) * 512],
                                       start=(hh == 0), stop=(hh == HT - 1))
                    pe.inc(li)
                    ck[f"pe_x1_{ft}_{half}"] = pe.v

            blk.tensor(ph8_pe_x1)

            def ph8_ve_q2(v, ft=ft, wr2=wr2):
                wq2_sb = wq2A if ft % 2 == 0 else wq2B
                if ft >= 2:
                    v.wait_ge(pe.h, ck[f"pe_x2_{ft - 2}_1"])
                quant(v, wr2, wq2_sb, 1)
                ck[f"ve_q2_{ft}"] = ve.v

            blk.vector(ph8_ve_q2)

            def ph8_pe_x2(te, ft=ft):
                wq2_sb = wq2A if ft % 2 == 0 else wq2B
                te.wait_ge(ve.h, ck[f"ve_q2_{ft}"])
                for half in range(2):
                    bkid = (4 * ft + 2 + half) % 7
                    prev = p8_lastuse.get(bkid)
                    if prev is not None:
                        te.wait_ge(prev[0], prev[1])
                    for hh in range(HT):
                        li = te.matmul(bank(bkid)[:, 0:512], wq2_sb[:, hh, :],
                                       y1T[:, hh, half * 512:(half + 1) * 512],
                                       start=(hh == 0), stop=(hh == HT - 1))
                    pe.inc(li)
                    ck[f"pe_x2_{ft}_{half}"] = pe.v

            blk.tensor(ph8_pe_x2)

            def ph8_sc(s, ft=ft):
                bks = [(4 * ft + j) % 7 for j in range(4)]
                s.wait_ge(pe.h, ck[f"pe_x1_{ft}_0"])
                if ft >= 1:
                    s.wait_ge(ve.h, ck[f"ve_h{ft - 1}_0"])
                sc.inc(s.activation(sgf0, bank(bks[0])[:, 0:512], AF.Sigmoid))
                ck[f"sc_sgf{ft}_0"] = sc.v
                s.wait_ge(pe.h, ck[f"pe_x1_{ft}_1"])
                if ft >= 1:
                    s.wait_ge(ve.h, ck[f"ve_h{ft - 1}_1"])
                sc.inc(s.activation(sgf1, bank(bks[1])[:, 0:512], AF.Sigmoid))
                ck[f"sc_sgf{ft}_1"] = sc.v

            blk.scalar(ph8_sc)
            p8_lastuse[(4 * ft) % 7] = (sc.h, ck[f"sc_sgf{ft}_0"])
            p8_lastuse[(4 * ft + 1) % 7] = (sc.h, ck[f"sc_sgf{ft}_1"])

            def ph8_ve_h(v, ft=ft):
                bks = [(4 * ft + j) % 7 for j in range(4)]
                v.wait_ge(sc.h, ck[f"sc_sgf{ft}_0"])
                v.wait_ge(pe.h, ck[f"pe_x2_{ft}_0"])
                ve.inc(v.tensor_mul(h_sb[:, ft, 0:512], sgf0, bank(bks[2])[:, 0:512]))
                ck[f"ve_h{ft}_0"] = ve.v
                v.wait_ge(sc.h, ck[f"sc_sgf{ft}_1"])
                v.wait_ge(pe.h, ck[f"pe_x2_{ft}_1"])
                ve.inc(v.tensor_mul(h_sb[:, ft, 512:1024], sgf1, bank(bks[3])[:, 0:512]))
                ck[f"ve_h{ft}_1"] = ve.v

            blk.vector(ph8_ve_h)
            p8_lastuse[(4 * ft + 2) % 7] = (ve.h, ck[f"ve_h{ft}_0"])
            p8_lastuse[(4 * ft + 3) % 7] = (ve.h, ck[f"ve_h{ft}_1"])

        if _DEBUG_TAPS:
            blk.gpsimd(lambda gp: (
                gp.wait_ge(ve.h, ck["ve_h0_1"]),
                d_dbg.dma(gp.dma_start(dbg["h0"], h_sb[:, 0, :]))))

        for dt in range(DT):
            wq3 = w3qA if dt % 2 == 0 else w3qB

            def ph9_sy(sy, dt=dt):
                if dt == 0:
                    sy.wait_ge(ve.h, ck[f"ve_q2_{FT - 1}"])
                else:
                    sy.wait_ge(ve.h, ck[f"ve_q3_{dt - 1}"])
                s_w3.dma(sy.dma_start(w3r.rearrange("p a t -> p (a t)"), d_w3[dt, :, :]))
                ck[f"sy_w3_{dt}"] = s_w3.v

            blk.sync(ph9_sy)

            def ph9_ve_q(v, dt=dt, wq3=wq3):
                v.wait_ge(s_w3.h, ck[f"sy_w3_{dt}"])
                if dt == 0:
                    v.wait_ge(ve.h, ck[f"ve_h{FT - 1}_1"])
                if dt == 1:
                    v.wait_ge(pe.h, ck[f"pe_x2_{FT - 1}_1"])
                if dt >= 2:
                    v.wait_ge(pe.h, ck[f"pe_f3_{dt - 2}"])
                src = w3r.rearrange("p a t -> p (a t)")
                dst = wq3.rearrange("p a t -> p (a t)")
                for a, b_ in ((0, 2048), (2048, 4096), (4096, 5632)):
                    w = b_ - a
                    v.wait_ge(ve.h, ve.v)
                    ve.inc(v.tensor_single_scalar(dst[:, a:b_], src[:, a:b_], nth_c[:, 2:3], ALU.is_le))
                    ve.inc(v.tensor_single_scalar(pq9[:, 0:w], src[:, a:b_], th_c[:, 2:3], ALU.is_ge))
                    v.wait_ge(ve.h, ve.v)
                    ve.inc(v.tensor_sub(dst[:, a:b_], pq9[:, 0:w], dst[:, a:b_]))
                ck[f"ve_q3_{dt}"] = ve.v

            blk.vector(ph9_ve_q)

            def ph9_pe(te, dt=dt, wq3=wq3):
                te.wait_ge(ve.h, ck[f"ve_q3_{dt}"])
                if dt >= 2:
                    te.wait_ge(ve.h, ck[f"ve_z2_{dt - 2}"])
                b0 = bank(0 if dt % 2 == 0 else 2)
                b1 = bank(1 if dt % 2 == 0 else 3)
                for ftl in range(FT):
                    te.matmul(b0[:, 0:512], wq3[:, ftl, :], h_sb[:, ftl, 0:512],
                              start=(ftl == 0), stop=(ftl == FT - 1))
                    li = te.matmul(b1[:, 0:512], wq3[:, ftl, :], h_sb[:, ftl, 512:1024],
                                   start=(ftl == 0), stop=(ftl == FT - 1))
                pe.inc(li)
                ck[f"pe_f3_{dt}"] = pe.v

            blk.tensor(ph9_pe)

            def ph9_ve_z2(v, dt=dt):
                v.wait_ge(pe.h, ck[f"pe_f3_{dt}"])
                b0 = bank(0 if dt % 2 == 0 else 2)
                b1 = bank(1 if dt % 2 == 0 else 3)
                v.scalar_tensor_tensor(z2[:, dt, 0:512], y1T[:, dt, 0:512], 1.0,
                                       b0[:, 0:512], ALU.mult, ALU.add)
                ve.inc(v.scalar_tensor_tensor(z2[:, dt, 512:1024], y1T[:, dt, 512:1024], 1.0,
                                              b1[:, 0:512], ALU.mult, ALU.add))
                ck[f"ve_z2_{dt}"] = ve.v

            blk.vector(ph9_ve_z2)

            def ph9_sc(s, dt=dt):
                s.wait_ge(ve.h, ck[f"ve_z2_{dt}"])
                if dt >= 1:
                    s.wait_ge(pe.h, ck[f"pe_l2s_{dt - 1}"])
                s.wait_ge(sc.h, sc.v)
                sc.inc(s.activation(junk9[:, 0:1024], z2[:, dt, :], AF.Square))
                ck[f"sc_z2sq{dt}"] = sc.v

            blk.scalar(ph9_sc)

            def ph9_pe_st(te, dt=dt):
                te.wait_ge(sc.h, ck[f"sc_z2sq{dt}"])
                te.matmul(ps[0:1, 2048:2560], ones_cb[:], z2[:, dt, 0:512],
                          start=(dt == 0), stop=(dt == DT - 1))
                te.matmul(ps[0:1, 2560:3072], ones_cb[:], z2[:, dt, 512:1024],
                          start=(dt == 0), stop=(dt == DT - 1))
                te.matmul(ps[0:1, 3072:3584], ones_cb[:], junk9[:, 0:512],
                          start=(dt == 0), stop=(dt == DT - 1))
                pe.inc(te.matmul(ps[0:1, 3584:4096], ones_cb[:], junk9[:, 512:1024],
                                 start=(dt == 0), stop=(dt == DT - 1)))
                ck[f"pe_l2s_{dt}"] = pe.v

            blk.tensor(ph9_pe_st)

        def ph9b_ve(v):
            v.wait_ge(pe.h, ck[f"pe_l2s_{DT - 1}"])
            v.tensor_copy(sums_r[0:1, :], ps[0:1, 2048:3072])
            ve.inc(v.tensor_copy(ssq_r[0:1, :], ps[0:1, 3072:4096]))
            ck["ve_l2rows"] = ve.v

        blk.vector(ph9b_ve)

        def ph9b_sc(s):
            s.wait_ge(ve.h, ck["ve_l2rows"])
            sc.inc(s.activation(ln2a[0:1, :], sums_r[0:1, :], AF.Copy, scale=1.0 / H))
            sc.inc(s.activation(ln2b[0:1, :], ssq_r[0:1, :], AF.Copy, scale=1.0 / H))
            s.wait_ge(sc.h, sc.v)
            sc.inc(s.activation(sums_r[0:1, :], ln2a[0:1, :], AF.Square))
            ck["sc_l2a"] = sc.v

        blk.scalar(ph9b_sc)

        def ph9b_ve2(v):
            v.wait_ge(sc.h, ck["sc_l2a"])
            ve.inc(v.tensor_sub(ssq_r[0:1, :], ln2b[0:1, :], sums_r[0:1, :]))
            ck["ve_l2b"] = ve.v

        blk.vector(ph9b_ve2)

        def ph9b_sc2(s):
            s.wait_ge(ve.h, ck["ve_l2b"])
            sc.inc(s.activation(ln2b[0:1, :], ssq_r[0:1, :], AF.Ln, bias=cst[0:1, 0:1]))
            s.wait_ge(sc.h, sc.v)
            sc.inc(s.activation(ssq_r[0:1, :], ln2b[0:1, :], AF.Exp, scale=-0.5))
            ck["sc_l2r"] = sc.v

        blk.scalar(ph9b_sc2)

        def ph9b_pe(te):
            te.wait_ge(sc.h, ck["sc_l2r"])
            te.matmul(bank(4)[:, 0:512], ones_rf[:], ln2a[0:1, 0:512], start=True, stop=True)
            te.matmul(bank(5)[:, 0:512], ones_rf[:], ln2a[0:1, 512:1024], start=True, stop=True)
            te.matmul(bank(6)[:, 0:512], ones_rf[:], ssq_r[0:1, 0:512], start=True, stop=True)
            pe.inc(te.matmul(bank(0)[:, 0:512], ones_rf[:], ssq_r[0:1, 512:1024], start=True, stop=True))
            ck["pe_l2b"] = pe.v

        blk.tensor(ph9b_pe)

        def ph9b_ve3(v):
            v.wait_ge(pe.h, ck["pe_l2b"])
            v.tensor_copy(mb_sb[:, 0:512], bank(4)[:, 0:512])
            v.tensor_copy(mb_sb[:, 512:1024], bank(5)[:, 0:512])
            v.tensor_copy(rb_sb[:, 0:512], bank(6)[:, 0:512])
            ve.inc(v.tensor_copy(rb_sb[:, 512:1024], bank(0)[:, 0:512]))
            ck["ve_l2bc"] = ve.v

        blk.vector(ph9b_ve3)

        if _DEBUG_TAPS:
            blk.gpsimd(lambda gp: (
                gp.wait_ge(ve.h, ck["ve_l2bc"]),
                d_dbg.dma(gp.dma_start(dbg["z2"], z2[:, 0, :]))))

        for dt in range(DT):
            def ph9c_ve(v, dt=dt):
                if dt == 0:
                    v.wait_ge(ve.h, ck["ve_l2bc"])
                else:
                    v.wait_ge(o_dma.h, ck[f"od_{dt - 1}"])
                ve.inc(v.scalar_tensor_tensor(y2t, z2[:, dt, :], 1.0, mb_sb, ALU.mult, ALU.subtract))
                v.wait_ge(ve.h, ve.v)
                ve.inc(v.tensor_mul(y2t, y2t, rb_sb))
                ck[f"ve_y2_{dt}"] = ve.v

            blk.vector(ph9c_ve)

            def ph9c_sy(sy, dt=dt):
                sy.wait_ge(ve.h, ck[f"ve_y2_{dt}"])
                o_dma.dma(sy.dma_start(d_y[dt * 128:(dt + 1) * 128, :], y2t))
                ck[f"od_{dt}"] = o_dma.v

            blk.sync(ph9c_sy)

        def fin_sync(sy):
            sy.wait_ge(o_dma.h, o_dma.v)

        blk.sync(fin_sync)

        def fin_gp(gp):
            gp.wait_ge(o_dma.h, o_dma.v)
            if _DEBUG_TAPS:
                gp.wait_ge(d_dbg.h, d_dbg.v)

        blk.gpsimd(fin_gp)

    return nc


_CACHED = None


def kernel(L0_attn_hidden_states, L0_attn_Wq, L0_attn_Wk, L0_attn_Wog,
           L0_attn_Wig, L0_attn_g_weight, L0_ffn_w1, L0_ffn_w2, L0_ffn_w3):
    global _CACHED
    x = np.ascontiguousarray(L0_attn_hidden_states, dtype=np.float32)
    wqT = np.ascontiguousarray(np.asarray(L0_attn_Wq, dtype=np.float32).T)
    wkT = np.ascontiguousarray(np.asarray(L0_attn_Wk, dtype=np.float32).T)
    def blkT(w, nb):
        wT = np.asarray(w, dtype=np.float32).T
        hin, hout = wT.shape
        bw = hout // nb
        return np.ascontiguousarray(
            wT.reshape(hin // 128, 128, nb, bw).transpose(2, 1, 0, 3).reshape(nb, 128, -1))
    wogB = blkT(L0_attn_Wog, NOS)
    wigB = blkT(L0_attn_Wig, NOS)
    w1Bh = blkT(L0_ffn_w1, FT)
    w2Bh = blkT(L0_ffn_w2, FT)
    w3Bh = blkT(L0_ffn_w3, DT)
    gw = np.ascontiguousarray(np.asarray(L0_attn_g_weight, dtype=np.float32))
    identb = np.eye(128, dtype=bfdt)
    masksb = np.triu(np.ones((64, 64), dtype=np.float32))

    in_maps = []
    for core in range(8):
        b, s = divmod(core, 4)
        t0 = s * T
        xs = np.ascontiguousarray(x[b, t0:t0 + T, :])
        xsT = np.ascontiguousarray(xs.T)
        xp = np.zeros((PRE, H), dtype=np.float32)
        if t0 > 0:
            xp[PRE - t0:] = x[b, 0:t0, :]
        xpT = np.ascontiguousarray(xp.T)
        in_maps.append({
            "x_seg": xs, "x_segT": xsT, "x_pre": xp, "x_preT": xpT,
            "wqT": wqT, "wkT": wkT, "wogB": wogB, "wigB": wigB,
            "w1B": w1Bh, "w2B": w2Bh, "w3B": w3Bh, "g_w": gw,
            "identb": identb, "masksb": masksb,
        })

    if _CACHED is None:
        _CACHED = _build()
    res = run_bass_kernel_spmd(_CACHED, in_maps, core_ids=list(range(8)))
    out = np.empty((B, L, H), dtype=np.float32)
    for core in range(8):
        b, s = divmod(core, 4)
        out[b, s * T:(s + 1) * T, :] = np.asarray(res.results[core]["y_out"]).T
    return out
